# version 15
# baseline (speedup 1.0000x reference)
import math
import numpy as np
from contextlib import ExitStack

import ml_dtypes
import concourse.bass as bass
import concourse.mybir as mybir
from concourse.bass_utils import run_bass_kernel_spmd

F32 = mybir.dt.float32
BF16 = mybir.dt.bfloat16
AF = mybir.ActivationFunctionType
OP = mybir.AluOpType
AX = mybir.AxisListType

C_SHIFT = 12.0


def build_program(cfg):
    S = cfg["S"]; E = cfg["E"]; H = cfg["H"]; KV = cfg["KV"]; D = cfg["D"]
    F = cfg["F"]; NL = cfg["NL"]; R = cfg["R"]
    OWN = S // 2
    EC = E // 128
    KC = S // 128
    PAIRS = KC // 2
    QG = max(OWN // 512, 1); QT = OWN // QG
    NIT = QG * H
    REP = H // KV
    QB = min(512, H * D); NQB = (H * D) // QB
    P1G = max(OWN // 512, 1); P1T = OWN // P1G
    T2 = min(512, S); NG2 = S // T2
    HALF = OWN // 2; NTH = HALF // 128
    FBW = 512; NFB = F // FBW
    RR = 2 * R
    LTW = OWN // 128
    HD = H * D; KVD = KV * D

    nc = bass.Bass("TRN2", target_bir_lowering=False)

    x_full = nc.dram_tensor("x_full", [S, E], F32, kind="ExternalInput")
    wq = nc.dram_tensor("wq", [E, HD], F32, kind="ExternalInput")
    bq = nc.dram_tensor("bq", [1, HD], F32, kind="ExternalInput")
    wk = nc.dram_tensor("wk", [E, KVD], F32, kind="ExternalInput")
    bk = nc.dram_tensor("bk", [1, KVD], F32, kind="ExternalInput")
    wv = nc.dram_tensor("wv", [E, KVD], F32, kind="ExternalInput")
    bv = nc.dram_tensor("bv", [1, KVD], F32, kind="ExternalInput")
    gate = nc.dram_tensor("gate", [1, NL], F32, kind="ExternalInput")
    wff = nc.dram_tensor("wff", [E, F], F32, kind="ExternalInput")
    bff = nc.dram_tensor("bff", [1, F], F32, kind="ExternalInput")
    lA = nc.dram_tensor("lA", [NL, R, E], F32, kind="ExternalInput")
    lB = nc.dram_tensor("lB", [NL, F, R], F32, kind="ExternalInput")
    lAs = nc.dram_tensor("lAs", [R, E], F32, kind="ExternalInput")
    lBs = nc.dram_tensor("lBs", [F, R], F32, kind="ExternalInput")
    id_bf = nc.dram_tensor("id_bf", [128, 128], BF16, kind="ExternalInput")
    id_f32 = nc.dram_tensor("id_f32", [128, 128], F32, kind="ExternalInput")
    pswap = nc.dram_tensor("pswap", [128, 128], BF16, kind="ExternalInput")
    cs2 = nc.dram_tensor("cs2", [128, S], F32, kind="ExternalInput")
    ss2 = nc.dram_tensor("ss2", [128, S], F32, kind="ExternalInput")
    out = nc.dram_tensor("out", [OWN, E], F32, kind="ExternalOutput")
    xn_tm = nc.dram_tensor("xn_tm", [S, E], BF16, kind="Internal")
    xnT_fm = nc.dram_tensor("xnT_fm", [E, S], BF16, kind="Internal")
    q_fm = nc.dram_tensor("q_fm", [HD, OWN], BF16, kind="Internal")
    k_own = nc.dram_tensor("k_own", [KVD, OWN], BF16, kind="Internal")
    v_own = nc.dram_tensor("v_own", [KV * OWN, 128], BF16, kind="Internal")
    k_ga = nc.dram_tensor("k_ga", [2 * KVD, OWN], BF16, kind="Internal")
    v_ga = nc.dram_tensor("v_ga", [2 * KV * OWN, 128], BF16, kind="Internal")
    attn_fm = nc.dram_tensor("attn_fm", [E, OWN], BF16, kind="Internal")
    xn2_tm = nc.dram_tensor("xn2_tm", [OWN, E], BF16, kind="Internal")

    xnT_view = xnT_fm.rearrange("(e p) s -> p e s", p=128)

    ctx = ExitStack()
    sb = lambda n, s, d: ctx.enter_context(nc.sbuf_tensor(n, s, d))
    sem = lambda n: ctx.enter_context(nc.semaphore(n))

    idb_sb = sb("idb_sb", [128, 128], BF16)
    idf_sb = sb("idf_sb", [128, 128], F32)
    psw_sb = sb("psw_sb", [128, 128], BF16)
    ones_col = sb("ones_col", [128, 1], BF16)
    ones_colf = sb("ones_colf", [128, 1], F32)
    ones_row = sb("ones_row", [1, 128], F32)
    c_scale = sb("c_scale", [128, 1], F32)
    c_bias = sb("c_bias", [128, 1], F32)
    c_eps = sb("c_eps", [128, 1], F32)
    c_nh = sb("c_nh", [128, 1], F32)
    c_invE = sb("c_invE", [128, 1], F32)
    c_one = sb("c_one", [128, 1], F32)
    c_negone = sb("c_negone", [128, 1], F32)
    l_tok = sb("l_tok", [128, H * LTW], F32)
    rl_tok = sb("rl_tok", [128, H * LTW], F32)
    bq_sb = sb("bq_sb", [128, H], F32)
    bk_sb = sb("bk_sb", [128, KV], F32)
    bv_b = sb("bv_b", [128, KVD], F32)

    with ExitStack() as pc:
        sbl = lambda n, s, d: pc.enter_context(nc.sbuf_tensor(n, s, d))
        psl = lambda n, s, d=F32: pc.enter_context(nc.psum_tensor(n, s, d))
        bqt = sbl("bqt", [H, 128], F32)
        bkt = sbl("bkt", [KV, 128], F32)
        bvrow = sbl("bvrow", [1, KVD], F32)
        ps_t = psl("i_ps_t", [128, 128])
        VBW = min(512, KVD)
        ps_b = psl("i_ps_b", [128, VBW])
        s_ini = sem("s_ini")
        s_ins = sem("s_ins")
        s_imm = sem("s_imm")
        s_iev = sem("s_iev")
        NVB = KVD // VBW
        with nc.Block() as blk:
            @blk.sync
            def _(sy):
                sy.dma_start(idb_sb[:], id_bf[:, :]).then_inc(s_ini, 16)
                sy.dma_start(idf_sb[:], id_f32[:, :]).then_inc(s_ini, 16)
                sy.dma_start(psw_sb[:], pswap[:, :]).then_inc(s_ini, 16)
                sy.dma_start(bvrow[:], bv[0:1, :]).then_inc(s_ini, 16)
                sy.dma_start(bqt[:], bq[0:1, :].rearrange("o (c d) -> (o c) d", d=128)).then_inc(s_ini, 16)
                sy.dma_start(bkt[:], bk[0:1, :].rearrange("o (c d) -> (o c) d", d=128)).then_inc(s_ini, 16)

            @blk.gpsimd
            def _(g):
                g.memset(ones_col[:], 1.0)
                g.memset(ones_colf[:], 1.0)
                g.memset(ones_row[:], 1.0)
                g.memset(c_scale[:], 1.0 / math.sqrt(D))
                g.memset(c_bias[:], -C_SHIFT)
                g.memset(c_eps[:], 1e-5)
                g.memset(c_nh[:], -0.5)
                g.memset(c_invE[:], 1.0 / E)
                g.memset(c_one[:], 1.0)
                g.memset(c_negone[:], -1.0).then_inc(s_ins, 1)

            @blk.tensor
            def _(t):
                t.wait_ge(s_ini, 16 * 6)
                t.wait_ge(s_ins, 1)
                t.transpose(ps_t[:, 0:H], bqt[:], idf_sb[0:H, 0:H]).then_inc(s_imm, 1)
                t.wait_ge(s_iev, 1)
                t.transpose(ps_t[:, 0:KV], bkt[:], idf_sb[0:KV, 0:KV]).then_inc(s_imm, 1)
                for j in range(NVB):
                    if j > 0:
                        t.wait_ge(s_iev, 2 + j)
                    t.matmul(ps_b[:, :], ones_row[:], bvrow[0:1, j * VBW:(j + 1) * VBW],
                             start=True, stop=True).then_inc(s_imm, 1)

            @blk.vector
            def _(v):
                v.wait_ge(s_imm, 1)
                v.tensor_copy(bq_sb[:], ps_t[:, 0:H]).then_inc(s_iev, 1)
                v.wait_ge(s_imm, 2)
                v.tensor_copy(bk_sb[:], ps_t[:, 0:KV]).then_inc(s_iev, 1)
                for j in range(NVB):
                    v.wait_ge(s_imm, 3 + j)
                    v.tensor_copy(bv_b[:, j * VBW:(j + 1) * VBW], ps_b[:, :]).then_inc(s_iev, 1)

    NT0 = S // 128
    NGT = S // 512
    cs_ctx = ExitStack()
    cs_sb = cs_ctx.enter_context(nc.sbuf_tensor("cs_sb", [128, S], F32))
    ss_sb = cs_ctx.enter_context(nc.sbuf_tensor("ss_sb", [128, S], F32))
    s_cs = sem("s_cs")
    with ExitStack() as pc:
        sbl = lambda n, s, d: pc.enter_context(nc.sbuf_tensor(n, s, d))
        xb = [sbl(f"p0_x{i}", [128, E], F32) for i in range(2)]
        xnb = [sbl(f"p0_xn{i}", [128, E], BF16) for i in range(2)]
        sq = sbl("p0_sq", [128, E], F32)
        stt = sbl("p0_st", [128, 8], F32)
        tb = [sbl(f"p0b_t{i}", [128, 4 * E], BF16) for i in range(2)]
        tp = [pc.enter_context(nc.psum_tensor(f"p0_tp{i}", [128, 512], BF16)) for i in range(2)]
        NPK = EC // 4
        s_x = [sem("p0_xa"), sem("p0_xb")]
        s_dv = sem("p0_dv")
        s_ac = sem("p0_ac")
        s_tr = sem("p0_tr")
        s_te = sem("p0_te")
        s_to = [sem("p0b_oa"), sem("p0b_ob")]
        with nc.Block() as blk:
            @blk.sync
            def _(sy):
                for i in range(NT0):
                    if i >= 2:
                        sy.wait_ge(s_ac, 3 * (i - 1))
                    sy.dma_start(xb[i % 2][:], x_full[i * 128:(i + 1) * 128, :]).then_inc(s_x[i % 2], 16)

            @blk.vector
            def _(v):
                for i in range(NT0):
                    v.wait_ge(s_x[i % 2], 16 * (i // 2 + 1))
                    if i >= 1:
                        v.wait_ge(s_dv, 6 * i)
                        v.wait_ge(s_ac, 3 * i)
                    v.scalar_tensor_tensor(sq[:], xb[i % 2][:], c_one[:, 0:1], xb[i % 2][:],
                                           OP.mult, OP.mult, accum_out=stt[:, 0:1]).then_inc(s_dv, 1)
                    v.reduce_sum(stt[:, 1:2], xb[i % 2][:], axis=AX.X).then_inc(s_dv, 1)
                    v.wait_ge(s_dv, 6 * i + 2)
                    v.tensor_scalar_mul(stt[:, 2:3], stt[:, 1:2], c_invE[:, 0:1]).then_inc(s_dv, 1)
                    v.wait_ge(s_dv, 6 * i + 3)
                    v.tensor_mul(stt[:, 3:4], stt[:, 2:3], stt[:, 2:3]).then_inc(s_dv, 1)
                    v.wait_ge(s_dv, 6 * i + 4)
                    v.scalar_tensor_tensor(stt[:, 4:5], stt[:, 0:1], c_invE[:, 0:1], stt[:, 3:4],
                                           OP.mult, OP.subtract).then_inc(s_dv, 1)
                    v.wait_ge(s_ac, 3 * i + 2)
                    v.scalar_tensor_tensor(stt[:, 7:8], stt[:, 2:3], c_negone[:, 0:1], stt[:, 6:7],
                                           OP.mult, OP.mult).then_inc(s_dv, 1)
                    n = i // 4
                    for pk in range(NPK):
                        gp = NPK * i + pk
                        if pk == 0 and i % 4 == 0 and n >= 2:
                            v.wait_ge(s_to[n % 2], 64 * (n // 2))
                        v.wait_ge(s_tr, gp + 1)
                        v.tensor_copy(tb[n % 2][:, (i % 4) * E + pk * 512:(i % 4) * E + (pk + 1) * 512],
                                      tp[gp % 2][:]).then_inc(s_te, 1)

            @blk.scalar
            def _(s):
                for i in range(NT0):
                    s.wait_ge(s_dv, 6 * i + 5)
                    s.activation(stt[:, 5:6], stt[:, 4:5], AF.Ln, bias=c_eps[:, 0:1]).then_inc(s_ac, 1)
                    s.wait_ge(s_ac, 3 * i + 1)
                    s.activation(stt[:, 6:7], stt[:, 5:6], AF.Exp, scale=c_nh[:, 0:1]).then_inc(s_ac, 1)
                    s.wait_ge(s_dv, 6 * i + 6)
                    s.wait_ge(s_x[i % 2], 16 * (i // 2 + 1))
                    if i >= 2:
                        s.wait_ge(s_tr, NPK * (i - 1))
                    s.activation(xnb[i % 2][:], xb[i % 2][:], AF.Identity,
                                 bias=stt[:, 7:8], scale=stt[:, 6:7]).then_inc(s_ac, 1)

            @blk.tensor
            def _(t):
                for i in range(NT0):
                    t.wait_ge(s_ac, 3 * i + 3)
                    for pk in range(NPK):
                        gp = NPK * i + pk
                        if gp >= 2:
                            t.wait_ge(s_te, gp - 1)
                        for e4 in range(4):
                            e = pk * 4 + e4
                            tr = t.transpose(tp[gp % 2][:, e4 * 128:(e4 + 1) * 128],
                                             xnb[i % 2][:, e * 128:(e + 1) * 128], idb_sb[:])
                        tr.then_inc(s_tr, 1)

            @blk.gpsimd
            def _(g):
                g.dma_start(cs_sb[:], cs2[:, :]).then_inc(s_cs, 16)
                g.dma_start(ss_sb[:], ss2[:, :]).then_inc(s_cs, 16)
                for n in range(NGT):
                    g.wait_ge(s_te, 4 * NPK * (n + 1))
                    for sub in range(4):
                        g.dma_start(
                            xnT_view[:, :, n * 512 + sub * 128: n * 512 + (sub + 1) * 128],
                            tb[n % 2][:, sub * E:(sub + 1) * E].rearrange("p (e c) -> p e c", c=128),
                        ).then_inc(s_to[n % 2], 16)
                g.wait_ge(s_cs, 32)
                g.wait_ge(s_to[0], 64 * ((NGT + 1) // 2))
                g.wait_ge(s_to[1], 64 * (NGT // 2))

    def proj_rope_phase(tag, NB, wsrc, WBC, NTG, T, bias_sb, head_of, spill_fn):
        NPCL = WBC // 128
        NPIECE = NB * NTG * NPCL
        NWB = min(NB, 2)

        def pidx(b, tg, qc):
            return (b * NTG + tg) * NPCL + qc

        with ExitStack() as pc:
            sbl = lambda n, s, d: pc.enter_context(nc.sbuf_tensor(n, s, d))
            psl = lambda n, s: pc.enter_context(nc.psum_tensor(n, s, F32))
            wbuf = [sbl(f"{tag}_w{i}", [128, EC * WBC], BF16) for i in range(NWB)]
            xnT = [sbl(f"{tag}_xnT{i}", [128, EC * T], BF16) for i in range(2)]
            qraw = [sbl(f"{tag}_qraw{i}", [128, T], BF16) for i in range(2)]
            t1b = [sbl(f"{tag}_t1{i}", [128, T], F32) for i in range(2)]
            t2b = [sbl(f"{tag}_t2{i}", [128, T], F32) for i in range(2)]
            rp = [sbl(f"{tag}_rp{i}", [128, T], BF16) for i in range(2)]
            ps_q = [psl(f"{tag}_psq{i}", [128, T]) for i in range(2)]
            ps_w = [psl(f"{tag}_psw{i}", [128, T]) for i in range(2)]
            s_w = [sem(f"{tag}_wa"), sem(f"{tag}_wb")]
            s_xT = [sem(f"{tag}_xTa"), sem(f"{tag}_xTb")]
            s_mm = sem(f"{tag}_mm")
            s_ev = sem(f"{tag}_ev")
            s_pm = sem(f"{tag}_pm")
            s_rp = sem(f"{tag}_rp")
            s_qs = [sem(f"{tag}_qsa"), sem(f"{tag}_qsb")]

            with nc.Block() as blk:
                @blk.gpsimd
                def _(g):
                    for b in range(NB):
                        if b >= NWB:
                            g.wait_ge(s_mm, (b - NWB + 1) * NTG * NPCL)
                        g.dma_start(
                            wbuf[b % NWB][:].rearrange("p (c q) -> p c q", q=WBC),
                            wsrc[:, b * WBC:(b + 1) * WBC].rearrange("(c p) q -> p c q", p=128),
                        ).then_inc(s_w[b % 2], 16)

                @blk.sync
                def _(sy):
                    for b in range(NB):
                        for tg in range(NTG):
                            n = b * NTG + tg
                            if n >= 2:
                                sy.wait_ge(s_mm, (n - 1) * NPCL)
                            sy.dma_start(
                                xnT[n % 2][:].rearrange("p (e t) -> p e t", t=T),
                                xnT_view[:, :, tg * T:(tg + 1) * T],
                            ).then_inc(s_xT[n % 2], 16)

                @blk.tensor
                def _(t):
                    for b in range(NB):
                        t.wait_ge(s_w[b % 2], 16 * (b // 2 + 1))
                        for tg in range(NTG):
                            n = b * NTG + tg
                            t.wait_ge(s_xT[n % 2], 16 * (n // 2 + 1))
                            for qc in range(NPCL):
                                pi = pidx(b, tg, qc)
                                if pi >= 2:
                                    t.wait_ge(s_ev, pi - 1)
                                for e in range(EC):
                                    mm = t.matmul(
                                        ps_q[pi % 2][:],
                                        wbuf[b % NWB][:, e * WBC + qc * 128: e * WBC + (qc + 1) * 128],
                                        xnT[n % 2][:, e * T:(e + 1) * T],
                                        start=(e == 0), stop=(e == EC - 1))
                                mm.then_inc(s_mm, 1)
                                t.wait_ge(s_ev, pi + 1)
                                if pi >= 2:
                                    t.wait_ge(s_rp, 3 * (pi - 2) + 2)
                                t.matmul(ps_w[pi % 2][:], psw_sb[:], qraw[pi % 2][:],
                                         start=True, stop=True).then_inc(s_pm, 1)

                @blk.scalar
                def _(s):
                    for b in range(NB):
                        for tg in range(NTG):
                            for qc in range(NPCL):
                                pi = pidx(b, tg, qc)
                                hd_i = head_of(b, qc)
                                s.wait_ge(s_mm, pi + 1)
                                if pi >= 2:
                                    s.wait_ge(s_pm, pi - 1)
                                    s.wait_ge(s_rp, 3 * (pi - 2) + 1)
                                s.activation(qraw[pi % 2][:], ps_q[pi % 2][:], AF.Identity,
                                             bias=bias_sb[:, hd_i:hd_i + 1]).then_inc(s_ev, 1)
                                s.wait_ge(s_rp, 3 * pi + 3)
                                s.dma_start(spill_fn(b, tg, qc), rp[pi % 2][:]).then_inc(s_qs[pi % 2], 16)

                @blk.vector
                def _(v):
                    for b in range(NB):
                        for tg in range(NTG):
                            for qc in range(NPCL):
                                pi = pidx(b, tg, qc)
                                tok0 = tg * T
                                v.wait_ge(s_ev, pi + 1)
                                if pi >= 2:
                                    v.wait_ge(s_rp, 3 * (pi - 2) + 3)
                                v.tensor_mul(t1b[pi % 2][:], qraw[pi % 2][:],
                                             cs_sb[:, tok0:tok0 + T]).then_inc(s_rp, 1)
                                v.wait_ge(s_pm, pi + 1)
                                v.tensor_mul(t2b[pi % 2][:], ps_w[pi % 2][:],
                                             ss_sb[:, tok0:tok0 + T]).then_inc(s_rp, 1)
                                v.wait_ge(s_rp, 3 * pi + 2)
                                if pi >= 2:
                                    v.wait_ge(s_qs[pi % 2], 16 * (pi // 2))
                                v.tensor_add(rp[pi % 2][:], t1b[pi % 2][:], t2b[pi % 2][:]).then_inc(s_rp, 1)

                @blk.gpsimd
                def _(g):
                    g.wait_ge(s_qs[0], 16 * ((NPIECE + 1) // 2))
                    g.wait_ge(s_qs[1], 16 * (NPIECE // 2))

    if True:
        proj_rope_phase(
            "p1", NQB, wq, QB, P1G, P1T,
            bias_sb=bq_sb,
            head_of=lambda b, qc: (b * QB) // 128 + qc,
            spill_fn=lambda b, tg, qc: q_fm[(b * QB + qc * 128):(b * QB + (qc + 1) * 128),
                                            tg * P1T:(tg + 1) * P1T],
        )
        proj_rope_phase(
            "p2a", 1, wk, KVD, OWN // T2, T2,
            bias_sb=bk_sb,
            head_of=lambda b, qc: qc,
            spill_fn=lambda b, tg, qc: k_own[qc * 128:(qc + 1) * 128, tg * T2:(tg + 1) * T2],
        )
    cs_ctx.close()

    with ExitStack() as pc:
        sbl = lambda n, s, d: pc.enter_context(nc.sbuf_tensor(n, s, d))
        psl = lambda n, s: pc.enter_context(nc.psum_tensor(n, s, F32))
        wvb = sbl("p2b_wv", [128, EC * KVD], BF16)
        xnT = [sbl(f"p2b_xnT{i}", [128, EC * T2], BF16) for i in range(2)]
        vstg = [sbl(f"p2b_vs{i}", [128, KVD], BF16) for i in range(2)]
        ps_v = [psl(f"p2b_ps{i}", [128, KVD]) for i in range(2)]
        s_w = sem("p2b_w")
        s_xT = [sem("p2b_xTa"), sem("p2b_xTb")]
        s_mm = sem("p2b_mm")
        s_ev = sem("p2b_ev")
        s_vs = [sem("p2b_vsa"), sem("p2b_vsb")]
        s_cc = sem("p2b_cc")
        NTC = T2 // 128
        NVN = KVD // 512 if KVD >= 512 else 1
        VNW = KVD // NVN
        NG2B = OWN // T2
        NPV = NG2B * NTC

        with nc.Block() as blk:
            @blk.gpsimd
            def _(g):
                g.dma_start(
                    wvb[:].rearrange("p (c q) -> p c q", q=KVD),
                    wv[:, :].rearrange("(c p) q -> p c q", p=128),
                ).then_inc(s_w, 16)

            @blk.sync
            def _(sy):
                for n in range(NG2B):
                    if n >= 2:
                        sy.wait_ge(s_mm, (n - 1) * NTC)
                    sy.dma_start(
                        xnT[n % 2][:].rearrange("p (e t) -> p e t", t=T2),
                        xnT_view[:, :, n * T2:(n + 1) * T2],
                    ).then_inc(s_xT[n % 2], 16)

            @blk.tensor
            def _(t):
                t.wait_ge(s_w, 16)
                for n in range(NG2B):
                    t.wait_ge(s_xT[n % 2], 16 * (n // 2 + 1))
                    for tc in range(NTC):
                        pi = n * NTC + tc
                        if pi >= 2:
                            t.wait_ge(s_ev, pi - 1)
                        for vn in range(NVN):
                            for e in range(EC):
                                mm = t.matmul(
                                    ps_v[pi % 2][:, vn * VNW:(vn + 1) * VNW],
                                    xnT[n % 2][:, e * T2 + tc * 128: e * T2 + (tc + 1) * 128],
                                    wvb[:, e * KVD + vn * VNW: e * KVD + (vn + 1) * VNW],
                                    start=(e == 0), stop=(e == EC - 1))
                        mm.then_inc(s_mm, 1)

            @blk.vector
            def _(v):
                for n in range(NG2B):
                    for tc in range(NTC):
                        pi = n * NTC + tc
                        v.wait_ge(s_mm, pi + 1)
                        if pi >= 2:
                            v.wait_ge(s_vs[pi % 2], 16 * KV * (pi // 2))
                        v.tensor_add(vstg[pi % 2][:], ps_v[pi % 2][:], bv_b[:, :]).then_inc(s_ev, 1)

            @blk.scalar
            def _(s):
                for n in range(NG2B):
                    for tc in range(NTC):
                        pi = n * NTC + tc
                        row0 = n * T2 + tc * 128
                        s.wait_ge(s_ev, pi + 1)
                        for hh in range(KV):
                            s.dma_start(v_own[hh * OWN + row0: hh * OWN + row0 + 128, :],
                                        vstg[pi % 2][:, hh * 128:(hh + 1) * 128]
                                        ).then_inc(s_vs[pi % 2], 16)

            @blk.gpsimd
            def _(g):
                g.wait_ge(s_vs[0], 16 * KV * ((NPV + 1) // 2))
                g.wait_ge(s_vs[1], 16 * KV * (NPV // 2))


    AcT = sb("AcT", [128, EC * RR], BF16)
    BcT = sb("BcT", [RR, F], BF16)
    bias_b = sb("bias_b", [128, F], F32)
    NOC = F // 128
    with ExitStack() as pc:
        sbl = lambda n, s, d: pc.enter_context(nc.sbuf_tensor(n, s, d))
        psl = lambda n, s, d=F32: pc.enter_context(nc.psum_tensor(n, s, d))
        grow = sbl("f0_g", [1, NL], F32)
        brow = sbl("f0_br", [1, F], F32)
        lab = [sbl(f"f0_la{i}", [R, E], F32) for i in range(2)]
        lbb = [sbl(f"f0_lb{i}", [128, NOC * R], F32) for i in range(2)]
        Aacc = sbl("f0_Aacc", [R, E], F32)
        Bacc = sbl("f0_Bacc", [128, NOC * R], F32)
        Asel = sbl("f0_Asel", [R, E], BF16)
        Ash = sbl("f0_Ash", [R, E], BF16)
        Bsel = sbl("f0_Bsel", [128, NOC * R], BF16)
        Bsh = sbl("f0_Bsh", [128, NOC * R], BF16)
        gate_b = sbl("f0_gb", [128, NL], F32)
        ps_g = psl("f0_psg", [128, 512])
        ps_A = [psl(f"f0_psA{i}", [128, R], BF16) for i in range(2)]
        ps_B = [psl(f"f0_psB{i}", [R, 128], BF16) for i in range(2)]
        s_ld = sem("f0_ld")
        s_la = [sem("f0_laa"), sem("f0_lab")]
        s_lbs = [sem("f0_lba"), sem("f0_lbb")]
        s_dv = sem("f0_dv")
        s_mm = sem("f0_mm")
        s_ev = sem("f0_ev")
        base0 = 1 + F // 512
        baseB = base0 + NL + 2
        dvtot = baseB + NL + 2
        with nc.Block() as blk:
            @blk.sync
            def _(sy):
                sy.dma_start(grow[:], gate[0:1, :]).then_inc(s_ld, 16)
                sy.dma_start(brow[:], bff[0:1, :]).then_inc(s_ld, 16)
                for n in range(NL):
                    if n >= 2:
                        sy.wait_ge(s_dv, base0 + n - 1)
                    sy.dma_start(lab[n % 2][:], lA[n, :, :]).then_inc(s_la[n % 2], 16)
                sy.wait_ge(s_dv, base0 + NL - 1)
                sy.dma_start(lab[NL % 2][:], lAs[:, :]).then_inc(s_la[NL % 2], 16)
                for n in range(NL):
                    if n >= 2:
                        sy.wait_ge(s_dv, baseB + n - 1)
                    sy.dma_start(
                        lbb[n % 2][:].rearrange("p (c r) -> p c r", r=R),
                        lB[n:n + 1, :, :].rearrange("a (c p) r -> (a p) c r", p=128),
                    ).then_inc(s_lbs[n % 2], 16)
                sy.wait_ge(s_dv, baseB + NL - 1)
                sy.dma_start(
                    lbb[NL % 2][:].rearrange("p (c r) -> p c r", r=R),
                    lBs[:, :].rearrange("(c p) r -> p c r", p=128),
                ).then_inc(s_lbs[NL % 2], 16)

            @blk.tensor
            def _(t):
                t.wait_ge(s_ld, 32)
                t.matmul(ps_g[:, 0:NL], ones_row[:], grow[:, :], start=True, stop=True).then_inc(s_mm, 1)
                for j in range(F // 512):
                    t.wait_ge(s_dv, j + 1)
                    t.matmul(ps_g[:, 0:512], ones_row[:], brow[0:1, j * 512:(j + 1) * 512],
                             start=True, stop=True).then_inc(s_mm, 1)
                t.wait_ge(s_dv, base0 + NL + 2)
                for e in range(EC):
                    for k, srcb in ((0, Asel), (1, Ash)):
                        pi = 2 * e + k
                        if pi >= 2:
                            t.wait_ge(s_ev, pi - 1)
                        t.transpose(ps_A[pi % 2][:, 0:R], srcb[0:R, e * 128:(e + 1) * 128],
                                    idb_sb[0:R, 0:R]).then_inc(s_mm, 1)
                t.wait_ge(s_dv, dvtot)
                for oc in range(NOC):
                    for k, srcb in ((0, Bsel), (1, Bsh)):
                        pi = 2 * EC + 2 * oc + k
                        t.wait_ge(s_ev, pi - 1)
                        t.transpose(ps_B[pi % 2][:, :], srcb[:, oc * R:(oc + 1) * R],
                                    idb_sb[:]).then_inc(s_mm, 1)

            @blk.vector
            def _(v):
                v.wait_ge(s_mm, 1)
                v.tensor_copy(gate_b[:], ps_g[:, 0:NL]).then_inc(s_dv, 1)
                for j in range(F // 512):
                    v.wait_ge(s_mm, j + 2)
                    v.tensor_copy(bias_b[:, j * 512:(j + 1) * 512], ps_g[:, 0:512]).then_inc(s_dv, 1)
                for n in range(NL):
                    v.wait_ge(s_la[n % 2], 16 * (n // 2 + 1))
                    if n == 0:
                        v.wait_ge(s_dv, 1)
                        v.tensor_scalar_mul(Aacc[:], lab[0][:], gate_b[0:R, 0:1]).then_inc(s_dv, 1)
                    else:
                        v.wait_ge(s_dv, base0 + n)
                        v.scalar_tensor_tensor(Aacc[:], lab[n % 2][:], gate_b[0:R, n:n + 1],
                                               Aacc[:], OP.mult, OP.add).then_inc(s_dv, 1)
                v.wait_ge(s_dv, base0 + NL)
                v.tensor_copy(Asel[:], Aacc[:]).then_inc(s_dv, 1)
                v.wait_ge(s_la[NL % 2], 16 * (NL // 2 + 1))
                v.tensor_copy(Ash[:], lab[NL % 2][:]).then_inc(s_dv, 1)
                for n in range(NL):
                    v.wait_ge(s_lbs[n % 2], 16 * (n // 2 + 1))
                    if n == 0:
                        v.tensor_scalar_mul(Bacc[:], lbb[0][:], gate_b[:, 0:1]).then_inc(s_dv, 1)
                    else:
                        v.wait_ge(s_dv, baseB + n)
                        v.scalar_tensor_tensor(Bacc[:], lbb[n % 2][:], gate_b[:, n:n + 1],
                                               Bacc[:], OP.mult, OP.add).then_inc(s_dv, 1)
                v.wait_ge(s_dv, baseB + NL)
                v.tensor_copy(Bsel[:], Bacc[:]).then_inc(s_dv, 1)
                v.wait_ge(s_lbs[NL % 2], 16 * (NL // 2 + 1))
                v.tensor_copy(Bsh[:], lbb[NL % 2][:]).then_inc(s_dv, 1)
                for e in range(EC):
                    for k in (0, 1):
                        pi = 2 * e + k
                        v.wait_ge(s_mm, 1 + F // 512 + pi + 1)
                        v.tensor_copy(AcT[:, e * RR + k * R: e * RR + (k + 1) * R],
                                      ps_A[pi % 2][:, 0:R]).then_inc(s_ev, 1)
                for oc in range(NOC):
                    for k in (0, 1):
                        pi = 2 * EC + 2 * oc + k
                        v.wait_ge(s_mm, 1 + F // 512 + pi + 1)
                        v.tensor_copy(BcT[k * R:(k + 1) * R, oc * 128:(oc + 1) * 128],
                                      ps_B[pi % 2][:, :]).then_inc(s_ev, 1)

            @blk.gpsimd
            def _(g):
                rg = [[2 * b_, 2 * b_ + 1] for b_ in range(4)]
                g.collective_compute("AllGather", OP.bypass, replica_groups=rg,
                                     ins=[k_own.ap().opt()],
                                     outs=[k_ga.ap().opt()]).then_inc(s_cc, 1)
                g.collective_compute("AllGather", OP.bypass, replica_groups=rg,
                                     ins=[v_own.ap().opt()],
                                     outs=[v_ga.ap().opt()]).then_inc(s_cc, 1)
                g.wait_ge(s_ev, 2 * EC + 2 * NOC)
                g.wait_ge(s_cc, 2)

    NTT = OWN // 128
    TPG = NTR = QT // 128
    HN = H * NTR
    with ExitStack() as pc:
        sbl = lambda n, s, d: pc.enter_context(nc.sbuf_tensor(n, s, d))
        psl = lambda n, s: pc.enter_context(nc.psum_tensor(n, s, F32))
        qbuf = [sbl(f"p3_q{i}", [128, QT], BF16) for i in range(2)]
        kbuf = [sbl(f"p3_k{i}", [128, S], BF16) for i in range(2)]
        vbuf = [sbl(f"p3_v{i}", [128, KC * 128], BF16) for i in range(2)]
        PT = [sbl(f"p3_pt{i}", [128, 1024], BF16) for i in range(4)]
        s_lb = [sbl(f"p3_sl{i}", [1, QT], F32) for i in range(2)]
        ostg = [sbl(f"p3_o{i}", [128, QT], BF16) for i in range(2)]
        xb = [sbl(f"f1_x{i}", [128, E], F32) for i in range(2)]
        atb = [sbl(f"f1_at{i}", [128, E], BF16) for i in range(2)]
        anb = [sbl(f"f1_an{i}", [128, E], BF16) for i in range(2)]
        xnb = [sbl(f"f1_xn{i}", [128, E], BF16) for i in range(2)]
        fsq = [sbl(f"f1_sq{i}", [128, E], F32) for i in range(2)]
        fst_t = sbl("f1_st", [128, 16], F32)
        STP = [psl(f"p3_stp{i}", [128, 1024]) for i in range(2)]
        OL = [psl(f"p3_ol{i}", [128, 1024]) for i in range(2)]
        s_qld = [sem("p3_qlda"), sem("p3_qldb")]
        s_kvl = [sem("p3_kvla"), sem("p3_kvlb")]
        s_stp = sem("p3_stp")
        s_exp = sem("p3_exp")
        s_pv = sem("p3_pv")
        s_oev = sem("p3_oev")
        s_lcp = sem("p3_lcp")
        s_ltr = sem("p3_ltr")
        s_lte = sem("p3_lte")
        s_asp = [sem("p3_aspa"), sem("p3_aspb")]
        s_x = [sem("f1_xa"), sem("f1_xb")]
        s_at = [sem("f1_ata"), sem("f1_atb")]
        s_dv = sem("f1_dv")
        s_ac = sem("f1_ac")
        s_rc = sem("f1_rc")
        s_ow = [sem("f1_owa"), sem("f1_owb")]
        s_sp = [sem("f1_spa"), sem("f1_spb")]
        KCH = KC // 2
        v_view = v_ga.rearrange("(x h c p) d -> x h p c d", x=2, p=128, c=KCH)

        def Rof(k):
            return (k // TPG + 1) * H + (k % TPG) * 6

        f1q = {"sync": [], "scalar": [], "vector": []}

        def sched(eng, r, fn):
            f1q[eng].append((r, fn))

        for k in range(NTT):
            grp_k = k // TPG
            sub_k = k % TPG
            Rk = Rof(k)
            row0 = k * 128

            def it_xld(sy, k=k, row0=row0):
                if k >= 2:
                    sy.wait_ge(s_ac, 3 * (k - 1))
                    sy.wait_ge(s_ow[k % 2], 16 * (k // 2))
                sy.dma_start(xb[k % 2][:], x_full[row0:row0 + 128, :]).then_inc(s_x[k % 2], 16)

            def it_atld(sy, k=k, row0=row0, grp_k=grp_k):
                if k >= 2:
                    sy.wait_ge(s_dv, 8 * (k - 1) + 1)
                sy.wait_ge(s_asp[0], 16 * (H // 2) * (grp_k + 1))
                sy.wait_ge(s_asp[1], 16 * (H // 2) * (grp_k + 1))
                sy.dma_start(atb[k % 2][:], attn_fm[:, row0:row0 + 128],
                             transpose=True).then_inc(s_at[k % 2], 16)

            def it_xout(sy, k=k, row0=row0):
                sy.wait_ge(s_dv, 8 * k + 2)
                sy.dma_start(out[row0:row0 + 128, :], xb[k % 2][:]).then_inc(s_ow[k % 2], 16)

            def it_spl(sy, k=k, row0=row0):
                sy.wait_ge(s_ac, 3 * k + 3)
                sy.dma_start(xn2_tm[row0:row0 + 128, :], xnb[k % 2][:]).then_inc(s_sp[k % 2], 16)

            sched("sync", Rk, it_atld)
            sched("sync", Rk, it_xld)
            sched("sync", Rk + 4, it_xout)
            sched("sync", Rk + 6, it_spl)

            def it_ln(s, k=k):
                o = 8 * (k % 2)
                fst = fst_t[:, o:o + 8]
                s.wait_ge(s_dv, 8 * k + 7)
                s.activation(fst[:, 5:6], fst[:, 4:5], AF.Ln, bias=c_eps[:, 0:1]).then_inc(s_ac, 1)
                s.wait_ge(s_ac, 3 * k + 1)
                s.activation(fst[:, 6:7], fst[:, 5:6], AF.Exp, scale=c_nh[:, 0:1]).then_inc(s_ac, 1)

            def it_xn(s, k=k):
                o = 8 * (k % 2)
                fst = fst_t[:, o:o + 8]
                s.wait_ge(s_dv, 8 * k + 8)
                if k >= 2:
                    s.wait_ge(s_sp[k % 2], 16 * (k // 2))
                s.activation(xnb[k % 2][:], xb[k % 2][:], AF.Identity,
                             bias=fst[:, 7:8], scale=fst[:, 6:7]).then_inc(s_ac, 1)

            sched("scalar", Rk + 4, it_ln)
            sched("scalar", Rk + 5, it_xn)

            def it_muls(v, k=k, grp_k=grp_k, sub_k=sub_k):
                if k % TPG == 0:
                    v.wait_ge(s_lte, (grp_k + 1) * H)
                    v.reciprocal(rl_tok[:, grp_k * HN:(grp_k + 1) * HN],
                                 l_tok[:, grp_k * HN:(grp_k + 1) * HN]).then_inc(s_rc, 1)
                v.wait_ge(s_rc, grp_k + 1)
                v.wait_ge(s_at[k % 2], 16 * (k // 2 + 1))
                if k >= 2:
                    v.wait_ge(s_dv, 8 * (k - 1))
                    v.wait_ge(s_ac, 3 * (k - 1))
                for g in range(H):
                    col = grp_k * HN + g * NTR + sub_k
                    mu = v.tensor_scalar_mul(anb[k % 2][:, g * 128:(g + 1) * 128],
                                             atb[k % 2][:, g * 128:(g + 1) * 128],
                                             rl_tok[:, col:col + 1])
                mu.then_inc(s_dv, 1)

            def it_add(v, k=k):
                v.wait_ge(s_x[k % 2], 16 * (k // 2 + 1))
                if k >= 2:
                    v.wait_ge(s_ow[k % 2], 16 * (k // 2))
                v.wait_ge(s_dv, 8 * k + 1)
                v.tensor_add(xb[k % 2][:], xb[k % 2][:], anb[k % 2][:]).then_inc(s_dv, 1)

            def it_stt(v, k=k):
                o = 8 * (k % 2)
                fst = fst_t[:, o:o + 8]
                v.wait_ge(s_dv, 8 * k + 2)
                v.scalar_tensor_tensor(fsq[k % 2][:], xb[k % 2][:], c_one[:, 0:1], xb[k % 2][:],
                                       OP.mult, OP.mult, accum_out=fst[:, 0:1]).then_inc(s_dv, 1)
                v.reduce_sum(fst[:, 1:2], xb[k % 2][:], axis=AX.X).then_inc(s_dv, 1)

            def it_stats(v, k=k):
                o = 8 * (k % 2)
                fst = fst_t[:, o:o + 8]
                v.wait_ge(s_dv, 8 * k + 4)
                v.tensor_scalar_mul(fst[:, 2:3], fst[:, 1:2], c_invE[:, 0:1]).then_inc(s_dv, 1)
                v.wait_ge(s_dv, 8 * k + 5)
                v.tensor_mul(fst[:, 3:4], fst[:, 2:3], fst[:, 2:3]).then_inc(s_dv, 1)
                v.wait_ge(s_dv, 8 * k + 6)
                v.scalar_tensor_tensor(fst[:, 4:5], fst[:, 0:1], c_invE[:, 0:1], fst[:, 3:4],
                                       OP.mult, OP.subtract).then_inc(s_dv, 1)

            def it_nmr(v, k=k):
                o = 8 * (k % 2)
                fst = fst_t[:, o:o + 8]
                v.wait_ge(s_ac, 3 * k + 2)
                v.scalar_tensor_tensor(fst[:, 7:8], fst[:, 2:3], c_negone[:, 0:1], fst[:, 6:7],
                                       OP.mult, OP.mult).then_inc(s_dv, 1)

            sched("vector", Rk, it_muls)
            sched("vector", Rk + 1, it_add)
            sched("vector", Rk + 2, it_stt)
            sched("vector", Rk + 3, it_stats)
            sched("vector", Rk + 3, it_nmr)

        f1cur = {"sync": 0, "scalar": 0, "vector": 0}

        def pump(eng, handle, i):
            lst = f1q[eng]
            c = f1cur[eng]
            while c < len(lst) and lst[c][0] <= i:
                lst[c][1](handle)
                c += 1
            f1cur[eng] = c

        def flush(eng, handle):
            for r, fn in f1q[eng][f1cur[eng]:]:
                fn(handle)
            f1cur[eng] = len(f1q[eng])

        def spill_attn(sy, j):
            grpj, gj = divmod(j, H)
            sy.wait_ge(s_oev, j + 1)
            sy.dma_start(attn_fm[gj * 128:(gj + 1) * 128, grpj * QT:(grpj + 1) * QT],
                         ostg[j % 2][:]).then_inc(s_asp[j % 2], 16)

        with nc.Block() as blk:
            @blk.sync
            def _(sy):
                for i in range(NIT):
                    grp, g = divmod(i, H)
                    if i >= 2:
                        sy.wait_ge(s_stp, (i - 1) * PAIRS)
                    sy.dma_start(qbuf[i % 2][:],
                                 q_fm[g * 128:(g + 1) * 128,
                                      grp * QT:(grp + 1) * QT]).then_inc(s_qld[i % 2], 16)
                    if g % REP == 0:
                        hkv = g // REP
                        L = grp * KV + hkv
                        if L >= 2:
                            sy.wait_ge(s_pv, (L - 1) * REP * PAIRS)
                        sy.dma_start(kbuf[L % 2][:, 0:OWN],
                                     k_ga[hkv * 128:(hkv + 1) * 128, :]).then_inc(s_kvl[L % 2], 16)
                        sy.dma_start(kbuf[L % 2][:, OWN:S],
                                     k_ga[KVD + hkv * 128: KVD + (hkv + 1) * 128, :]).then_inc(s_kvl[L % 2], 16)
                        sy.dma_start(vbuf[L % 2][:, 0:KCH * 128].rearrange("p (c d) -> p c d", d=128),
                                     v_view[0, hkv]).then_inc(s_kvl[L % 2], 16)
                        sy.dma_start(vbuf[L % 2][:, KCH * 128:].rearrange("p (c d) -> p c d", d=128),
                                     v_view[1, hkv]).then_inc(s_kvl[L % 2], 16)
                    if i >= 1:
                        spill_attn(sy, i - 1)
                    pump("sync", sy, i)
                spill_attn(sy, NIT - 1)
                flush("sync", sy)

            @blk.tensor
            def _(t):
                def emit_pv(i, p, L):
                    G = i * PAIRS + p
                    t.wait_ge(s_exp, G + 1)
                    if p == 0 and i >= 2:
                        t.wait_ge(s_oev, i - 1)
                        t.wait_ge(s_lcp, i - 1)
                        t.wait_ge(s_lte, i - 1)
                    for jj in (2 * p, 2 * p + 1):
                        pts = PT[G % 4][:, (jj % 2) * 512:(jj % 2) * 512 + QT]
                        t.matmul(OL[i % 2][:, 0:QT],
                                 vbuf[L % 2][:, jj * 128:(jj + 1) * 128],
                                 pts, start=(jj == 0), stop=(jj == KC - 1))
                        last = t.matmul(OL[i % 2][0:1, 512:512 + QT],
                                        ones_col[:], pts,
                                        start=(jj == 0), stop=(jj == KC - 1))
                    last.then_inc(s_pv, 1)

                def emit_lt(j):
                    t.wait_ge(s_lcp, j + 1)
                    t.wait_ge(s_lte, max(j - 1, 0))
                    for c in range(NTR):
                        tr = t.transpose(OL[j % 2][:, 516 + c:517 + c],
                                         s_lb[j % 2][0:1, c * 128:(c + 1) * 128], idf_sb[0:1, 0:1])
                    tr.then_inc(s_ltr, 1)

                for i in range(NIT):
                    grp, g = divmod(i, H)
                    hkv = g // REP
                    L = grp * KV + hkv
                    t.wait_ge(s_qld[i % 2], 16 * (i // 2 + 1))
                    if g % REP == 0:
                        t.wait_ge(s_kvl[L % 2], 64 * (L // 2 + 1))
                    for p in range(PAIRS):
                        G = i * PAIRS + p
                        if G >= 2:
                            t.wait_ge(s_exp, G - 1)
                        for jj in (2 * p, 2 * p + 1):
                            mm = t.matmul(STP[p % 2][:, (jj % 2) * 512:(jj % 2) * 512 + QT],
                                          kbuf[L % 2][:, jj * 128:(jj + 1) * 128],
                                          qbuf[i % 2][:],
                                          start=True, stop=True)
                        mm.then_inc(s_stp, 1)
                        if p >= 1:
                            emit_pv(i, p - 1, L)
                        if p == 2 and i >= 1:
                            emit_lt(i - 1)
                    emit_pv(i, PAIRS - 1, L)
                emit_lt(NIT - 1)

            @blk.scalar
            def _(s):
                for i in range(NIT):
                    grp, g = divmod(i, H)
                    for p in range(PAIRS):
                        G = i * PAIRS + p
                        s.wait_ge(s_stp, G + 1)
                        if G >= 4:
                            s.wait_ge(s_pv, G - 3)
                        s.activation(PT[G % 4][:, 0:1024], STP[p % 2][:, 0:1024], AF.Exp,
                                     scale=c_scale[:, 0:1], bias=c_bias[:, 0:1]).then_inc(s_exp, 1)
                    pump("scalar", s, i)
                flush("scalar", s)

            @blk.vector
            def _(v):
                def emit_ltev(j):
                    grpj, gj = divmod(j, H)
                    v.wait_ge(s_ltr, j + 1)
                    col = grpj * HN + gj * NTR
                    v.tensor_copy(l_tok[:, col:col + NTR],
                                  OL[j % 2][:, 516:516 + NTR]).then_inc(s_lte, 1)

                for i in range(NIT):
                    grp, g = divmod(i, H)
                    v.wait_ge(s_pv, (i + 1) * PAIRS)
                    if i >= 2:
                        v.wait_ge(s_asp[i % 2], 16 * (i // 2))
                    v.tensor_copy(ostg[i % 2][:], OL[i % 2][:, 0:QT]).then_inc(s_oev, 1)
                    if i >= 2:
                        v.wait_ge(s_ltr, i - 1)
                    v.tensor_copy(s_lb[i % 2][:], OL[i % 2][0:1, 512:512 + QT]).then_inc(s_lcp, 1)
                    if i >= 1:
                        emit_ltev(i - 1)
                    pump("vector", v, i)
                emit_ltev(NIT - 1)
                flush("vector", v)

            @blk.gpsimd
            def _(g):
                g.wait_ge(s_asp[0], 16 * ((NIT + 1) // 2))
                g.wait_ge(s_asp[1], 16 * (NIT // 2))
                g.wait_ge(s_ow[0], 16 * ((NTT + 1) // 2))
                g.wait_ge(s_ow[1], 16 * (NTT // 2))
                g.wait_ge(s_sp[0], 16 * ((NTT + 1) // 2))
                g.wait_ge(s_sp[1], 16 * (NTT // 2))

    with ExitStack() as pc:
        sbl = lambda n, s, d: pc.enter_context(nc.sbuf_tensor(n, s, d))
        psl = lambda n, s: pc.enter_context(nc.psum_tensor(n, s, F32))
        xn2T = sbl("f3_xn2T", [128, EC * HALF], BF16)
        kb = [sbl(f"f3_kb{i}", [128, EC * FBW], BF16) for i in range(2)]
        AxT = sbl("f3_AxT", [RR, HALF], BF16)
        osb = [sbl(f"f3_o{i}", [128, FBW], F32) for i in range(2)]
        ps_f = [psl(f"f3_ps{i}", [128, FBW]) for i in range(2)]
        SW = min(512, HALF)
        ps_ax = psl("f3_psax", [RR, SW])
        s_xT = [sem("f3_xTa"), sem("f3_xTb")]
        s_kb = [sem("f3_kba"), sem("f3_kbb")]
        s_ax = sem("f3_ax")
        s_mm = sem("f3_mm")
        s_ev = sem("f3_ev")
        s_ga = [sem("f3_gaa"), sem("f3_gab")]
        NSEG = HALF // SW
        NPH = NFB * NTH
        NFB2 = 2 * NFB

        def kbl(g_or_sy, fbg):
            hf2, fb2 = divmod(fbg, NFB)
            g_or_sy.dma_start(
                kb[fbg % 2][:].rearrange("p (c q) -> p c q", q=FBW),
                wff[:, fb2 * FBW:(fb2 + 1) * FBW].rearrange("(c p) q -> p c q", p=128),
            ).then_inc(s_kb[fbg % 2], 16)

        with nc.Block() as blk:
            @blk.sync
            def _(sy):
                for hf in range(2):
                    if hf == 1:
                        sy.wait_ge(s_mm, NPH)
                        sy.wait_ge(s_ax, 3 * NSEG)
                    for e in range(EC):
                        sy.dma_start(
                            xn2T[:, e * HALF:(e + 1) * HALF],
                            xn2_tm[hf * HALF:(hf + 1) * HALF, e * 128:(e + 1) * 128],
                            transpose=True,
                        ).then_inc(s_xT[hf], 16)

            @blk.gpsimd
            def _(g):
                for fbg in range(min(2, NFB2)):
                    kbl(g, fbg)
                for hf in range(2):
                    for fb in range(NFB):
                        for ti in range(NTH):
                            n3 = (hf * NFB + fb) * NTH + ti
                            g.wait_ge(s_ev, n3 + 1)
                            row0 = hf * HALF + ti * 128
                            g.dma_start(out[row0:row0 + 128, fb * FBW:(fb + 1) * FBW],
                                        osb[n3 % 2][:],
                                        accum_op=OP.add).then_inc(s_ga[n3 % 2], 16)
                        fbg = hf * NFB + fb
                        if fbg + 2 < NFB2:
                            g.wait_ge(s_mm, (fbg + 1) * NTH)
                            kbl(g, fbg + 2)

            @blk.tensor
            def _(t):
                for hf in range(2):
                    t.wait_ge(s_xT[hf], 16 * EC)
                    for sg in range(NSEG):
                        sgg = hf * NSEG + sg
                        if sgg >= 1:
                            t.wait_ge(s_ax, 3 * sgg)
                        for e in range(EC):
                            mm = t.matmul(ps_ax[:, :],
                                          AcT[:, e * RR:(e + 1) * RR],
                                          xn2T[:, e * HALF + sg * SW: e * HALF + (sg + 1) * SW],
                                          start=(e == 0), stop=(e == EC - 1))
                        mm.then_inc(s_ax, 2)
                    t.wait_ge(s_ax, 3 * NSEG * (hf + 1))
                    for fb in range(NFB):
                        fbg = hf * NFB + fb
                        t.wait_ge(s_kb[fbg % 2], 16 * (fbg // 2 + 1))
                        for ti in range(NTH):
                            n3 = fbg * NTH + ti
                            if n3 >= 2:
                                t.wait_ge(s_ev, n3 - 1)
                            for e in range(EC):
                                t.matmul(ps_f[n3 % 2][:],
                                         xn2T[:, e * HALF + ti * 128: e * HALF + (ti + 1) * 128],
                                         kb[fbg % 2][:, e * FBW:(e + 1) * FBW],
                                         start=(e == 0), stop=False)
                            t.matmul(ps_f[n3 % 2][:],
                                     AxT[:, ti * 128:(ti + 1) * 128],
                                     BcT[:, fb * FBW:(fb + 1) * FBW],
                                     start=False, stop=True).then_inc(s_mm, 1)

            @blk.vector
            def _(v):
                for hf in range(2):
                    for sg in range(NSEG):
                        sgg = hf * NSEG + sg
                        v.wait_ge(s_ax, 3 * sgg + 2)
                        if hf == 1 and sg == 0:
                            v.wait_ge(s_mm, NPH)
                        v.tensor_copy(AxT[:, sg * SW:(sg + 1) * SW], ps_ax[:, :]).then_inc(s_ax, 1)
                    for fb in range(NFB):
                        for ti in range(NTH):
                            n3 = (hf * NFB + fb) * NTH + ti
                            v.wait_ge(s_mm, n3 + 1)
                            if n3 >= 2:
                                v.wait_ge(s_ga[n3 % 2], 16 * (n3 // 2))
                            v.tensor_add(osb[n3 % 2][:], ps_f[n3 % 2][:],
                                         bias_b[:, fb * FBW:(fb + 1) * FBW]).then_inc(s_ev, 1)

            @blk.gpsimd
            def _(g):
                npc = 2 * NPH
                g.wait_ge(s_ga[0], 16 * ((npc + 1) // 2))
                g.wait_ge(s_ga[1], 16 * (npc // 2))

    ctx.close()
    return nc


_PROG_CACHE = {}


def _get_program(cfg_key, cfg):
    if cfg_key not in _PROG_CACHE:
        _PROG_CACHE[cfg_key] = build_program(cfg)
    return _PROG_CACHE[cfg_key]


def _rope_tables(S, D, shift):
    inv = 10000.0 ** (-np.arange(0, D, 2, dtype=np.float64) / D)
    pos = (np.arange(S, dtype=np.int64) + shift) % S
    ang = pos[None, :].astype(np.float64) * inv[:, None]
    cos = np.cos(ang).astype(np.float32)
    sin = np.sin(ang).astype(np.float32)
    cs2 = np.concatenate([cos, cos], axis=0)
    ss2 = np.concatenate([-sin, sin], axis=0)
    return np.ascontiguousarray(cs2), np.ascontiguousarray(ss2)


def kernel(L0_attn_x, L0_attn_wq, L0_attn_bq, L0_attn_wk, L0_attn_bk,
           L0_attn_wv, L0_attn_bv, L0_ffn_gate, L0_ffn_kernel, L0_ffn_bias,
           L0_ffn_lora_A, L0_ffn_lora_B, L0_ffn_lora_A_shared,
           L0_ffn_lora_B_shared):
    x = np.asarray(L0_attn_x)
    B, S, E = x.shape
    H = np.asarray(L0_attn_wq).shape[1] // 128
    KV = np.asarray(L0_attn_wk).shape[1] // 128
    F = np.asarray(L0_ffn_kernel).shape[1]
    NL, R, _ = np.asarray(L0_ffn_lora_A).shape
    cfg = dict(S=S, E=E, H=H, KV=KV, D=128, F=F, NL=NL, R=R)
    OWN = S // 2
    nc = _get_program((S, E, H, KV, F, NL, R), cfg)

    f32c = lambda a: np.ascontiguousarray(np.asarray(a, dtype=np.float32))
    wq_ = f32c(L0_attn_wq); bq_ = f32c(L0_attn_bq).reshape(1, -1)
    wk_ = f32c(L0_attn_wk); bk_ = f32c(L0_attn_bk).reshape(1, -1)
    wv_ = f32c(L0_attn_wv); bv_ = f32c(L0_attn_bv).reshape(1, -1)
    wff_ = f32c(L0_ffn_kernel); bff_ = f32c(L0_ffn_bias).reshape(1, -1)
    gate_ = f32c(L0_ffn_gate)
    lA_ = f32c(L0_ffn_lora_A); lB_ = f32c(L0_ffn_lora_B)
    lAs_ = f32c(L0_ffn_lora_A_shared); lBs_ = f32c(L0_ffn_lora_B_shared)
    id_f = np.eye(128, dtype=np.float32)
    id_b = id_f.astype(ml_dtypes.bfloat16)
    psw = np.zeros((128, 128), dtype=np.float32)
    for m in range(128):
        psw[(m + 64) % 128, m] = 1.0
    psw = psw.astype(ml_dtypes.bfloat16)

    in_maps = []
    for c in range(8):
        b, h = divmod(c, 2)
        xc = np.ascontiguousarray(np.roll(f32c(x[b]), -h * OWN, axis=0))
        cs2_, ss2_ = _rope_tables(S, 128, h * OWN)
        in_maps.append({
            "x_full": xc, "wq": wq_, "bq": bq_, "wk": wk_, "bk": bk_,
            "wv": wv_, "bv": bv_, "gate": np.ascontiguousarray(gate_[c:c + 1, :]),
            "wff": wff_, "bff": bff_, "lA": lA_, "lB": lB_, "lAs": lAs_, "lBs": lBs_,
            "id_bf": id_b, "id_f32": id_f, "pswap": psw, "cs2": cs2_, "ss2": ss2_,
        })
    res = run_bass_kernel_spmd(nc, in_maps, core_ids=list(range(8)))
    y = np.empty((B, S, E), dtype=np.float32)
    for c in range(8):
        b, h = divmod(c, 2)
        y[b, h * OWN:(h + 1) * OWN, :] = res.results[c]["out"]
    return y


# version 16 (final)
# speedup vs baseline: 1.0503x; 1.0503x over previous
import math
import numpy as np
from contextlib import ExitStack

import ml_dtypes
import concourse.bass as bass
import concourse.mybir as mybir
from concourse.bass_utils import run_bass_kernel_spmd

F32 = mybir.dt.float32
BF16 = mybir.dt.bfloat16
AF = mybir.ActivationFunctionType
OP = mybir.AluOpType
AX = mybir.AxisListType

C_SHIFT = 12.0


def build_program(cfg):
    S = cfg["S"]; E = cfg["E"]; H = cfg["H"]; KV = cfg["KV"]; D = cfg["D"]
    F = cfg["F"]; NL = cfg["NL"]; R = cfg["R"]
    OWN = S // 2
    EC = E // 128
    KC = S // 128
    PAIRS = KC // 2
    QG = max(OWN // 512, 1); QT = OWN // QG
    NIT = QG * H
    REP = H // KV
    QB = min(512, H * D); NQB = (H * D) // QB
    P1G = max(OWN // 512, 1); P1T = OWN // P1G
    T2 = min(512, S); NG2 = S // T2
    HALF = OWN // 2; NTH = HALF // 128
    FBW = 512; NFB = F // FBW
    RR = 2 * R
    LTW = OWN // 128
    HD = H * D; KVD = KV * D

    nc = bass.Bass("TRN2", target_bir_lowering=False)

    x_full = nc.dram_tensor("x_full", [S, E], F32, kind="ExternalInput")
    wq = nc.dram_tensor("wq", [E, HD], F32, kind="ExternalInput")
    bq = nc.dram_tensor("bq", [1, HD], F32, kind="ExternalInput")
    wk = nc.dram_tensor("wk", [E, KVD], F32, kind="ExternalInput")
    bk = nc.dram_tensor("bk", [1, KVD], F32, kind="ExternalInput")
    wv = nc.dram_tensor("wv", [E, KVD], F32, kind="ExternalInput")
    bv = nc.dram_tensor("bv", [1, KVD], F32, kind="ExternalInput")
    gate = nc.dram_tensor("gate", [1, NL], F32, kind="ExternalInput")
    wff = nc.dram_tensor("wff", [E, F], F32, kind="ExternalInput")
    bff = nc.dram_tensor("bff", [1, F], F32, kind="ExternalInput")
    lA = nc.dram_tensor("lA", [NL, R, E], F32, kind="ExternalInput")
    lB = nc.dram_tensor("lB", [NL, F, R], F32, kind="ExternalInput")
    lAs = nc.dram_tensor("lAs", [R, E], F32, kind="ExternalInput")
    lBs = nc.dram_tensor("lBs", [F, R], F32, kind="ExternalInput")
    id_bf = nc.dram_tensor("id_bf", [128, 128], BF16, kind="ExternalInput")
    id_f32 = nc.dram_tensor("id_f32", [128, 128], F32, kind="ExternalInput")
    pswap = nc.dram_tensor("pswap", [128, 128], BF16, kind="ExternalInput")
    cs2 = nc.dram_tensor("cs2", [128, S], F32, kind="ExternalInput")
    ss2 = nc.dram_tensor("ss2", [128, S], F32, kind="ExternalInput")
    out = nc.dram_tensor("out", [OWN, E], F32, kind="ExternalOutput")
    xn_tm = nc.dram_tensor("xn_tm", [S, E], BF16, kind="Internal")
    xnT_fm = nc.dram_tensor("xnT_fm", [E, S], BF16, kind="Internal")
    q_fm = nc.dram_tensor("q_fm", [HD, OWN], BF16, kind="Internal")
    k_own = nc.dram_tensor("k_own", [KVD, OWN], BF16, kind="Internal")
    v_own = nc.dram_tensor("v_own", [KV * OWN, 128], BF16, kind="Internal")
    k_ga = nc.dram_tensor("k_ga", [2 * KVD, OWN], BF16, kind="Internal")
    v_ga = nc.dram_tensor("v_ga", [2 * KV * OWN, 128], BF16, kind="Internal")
    attn_fm = nc.dram_tensor("attn_fm", [E, OWN], BF16, kind="Internal")
    xn2_tm = nc.dram_tensor("xn2_tm", [OWN, E], BF16, kind="Internal")

    xnT_view = xnT_fm.rearrange("(e p) s -> p e s", p=128)

    ctx = ExitStack()
    sb = lambda n, s, d: ctx.enter_context(nc.sbuf_tensor(n, s, d))
    sem = lambda n: ctx.enter_context(nc.semaphore(n))

    idb_sb = sb("idb_sb", [128, 128], BF16)
    idf_sb = sb("idf_sb", [128, 128], F32)
    psw_sb = sb("psw_sb", [128, 128], BF16)
    ones_col = sb("ones_col", [128, 1], BF16)
    ones_colf = sb("ones_colf", [128, 1], F32)
    ones_row = sb("ones_row", [1, 128], F32)
    c_scale = sb("c_scale", [128, 1], F32)
    c_bias = sb("c_bias", [128, 1], F32)
    c_eps = sb("c_eps", [128, 1], F32)
    c_nh = sb("c_nh", [128, 1], F32)
    c_invE = sb("c_invE", [128, 1], F32)
    c_one = sb("c_one", [128, 1], F32)
    c_negone = sb("c_negone", [128, 1], F32)
    l_tok = sb("l_tok", [128, H * LTW], F32)
    rl_tok = sb("rl_tok", [128, H * LTW], F32)
    bq_sb = sb("bq_sb", [128, H], F32)
    bk_sb = sb("bk_sb", [128, KV], F32)
    bv_b = sb("bv_b", [128, KVD], F32)

    with ExitStack() as pc:
        sbl = lambda n, s, d: pc.enter_context(nc.sbuf_tensor(n, s, d))
        psl = lambda n, s, d=F32: pc.enter_context(nc.psum_tensor(n, s, d))
        bqt = sbl("bqt", [H, 128], F32)
        bkt = sbl("bkt", [KV, 128], F32)
        bvrow = sbl("bvrow", [1, KVD], F32)
        ps_t = psl("i_ps_t", [128, 128])
        VBW = min(512, KVD)
        ps_b = psl("i_ps_b", [128, VBW])
        s_ini = sem("s_ini")
        s_ins = sem("s_ins")
        s_imm = sem("s_imm")
        s_iev = sem("s_iev")
        NVB = KVD // VBW
        with nc.Block() as blk:
            @blk.sync
            def _(sy):
                sy.dma_start(idb_sb[:], id_bf[:, :]).then_inc(s_ini, 16)
                sy.dma_start(idf_sb[:], id_f32[:, :]).then_inc(s_ini, 16)
                sy.dma_start(psw_sb[:], pswap[:, :]).then_inc(s_ini, 16)
                sy.dma_start(bvrow[:], bv[0:1, :]).then_inc(s_ini, 16)
                sy.dma_start(bqt[:], bq[0:1, :].rearrange("o (c d) -> (o c) d", d=128)).then_inc(s_ini, 16)
                sy.dma_start(bkt[:], bk[0:1, :].rearrange("o (c d) -> (o c) d", d=128)).then_inc(s_ini, 16)

            @blk.gpsimd
            def _(g):
                g.memset(ones_col[:], 1.0)
                g.memset(ones_colf[:], 1.0)
                g.memset(ones_row[:], 1.0)
                g.memset(c_scale[:], 1.0 / math.sqrt(D))
                g.memset(c_bias[:], -C_SHIFT)
                g.memset(c_eps[:], 1e-5)
                g.memset(c_nh[:], -0.5)
                g.memset(c_invE[:], 1.0 / E)
                g.memset(c_one[:], 1.0)
                g.memset(c_negone[:], -1.0).then_inc(s_ins, 1)

            @blk.tensor
            def _(t):
                t.wait_ge(s_ini, 16 * 6)
                t.wait_ge(s_ins, 1)
                t.transpose(ps_t[:, 0:H], bqt[:], idf_sb[0:H, 0:H]).then_inc(s_imm, 1)
                t.wait_ge(s_iev, 1)
                t.transpose(ps_t[:, 0:KV], bkt[:], idf_sb[0:KV, 0:KV]).then_inc(s_imm, 1)
                for j in range(NVB):
                    if j > 0:
                        t.wait_ge(s_iev, 2 + j)
                    t.matmul(ps_b[:, :], ones_row[:], bvrow[0:1, j * VBW:(j + 1) * VBW],
                             start=True, stop=True).then_inc(s_imm, 1)

            @blk.vector
            def _(v):
                v.wait_ge(s_imm, 1)
                v.tensor_copy(bq_sb[:], ps_t[:, 0:H]).then_inc(s_iev, 1)
                v.wait_ge(s_imm, 2)
                v.tensor_copy(bk_sb[:], ps_t[:, 0:KV]).then_inc(s_iev, 1)
                for j in range(NVB):
                    v.wait_ge(s_imm, 3 + j)
                    v.tensor_copy(bv_b[:, j * VBW:(j + 1) * VBW], ps_b[:, :]).then_inc(s_iev, 1)

    NT0 = S // 128
    NGT = S // 512
    cs_ctx = ExitStack()
    cs_sb = cs_ctx.enter_context(nc.sbuf_tensor("cs_sb", [128, S], F32))
    ss_sb = cs_ctx.enter_context(nc.sbuf_tensor("ss_sb", [128, S], F32))
    s_cs = sem("s_cs")
    with ExitStack() as pc:
        sbl = lambda n, s, d: pc.enter_context(nc.sbuf_tensor(n, s, d))
        xb = [sbl(f"p0_x{i}", [128, E], F32) for i in range(2)]
        xnb = [sbl(f"p0_xn{i}", [128, E], BF16) for i in range(2)]
        sq = [sbl(f"p0_sq{i}", [128, E], F32) for i in range(2)]
        stt_t = sbl("p0_st", [128, 16], F32)
        tb = [sbl(f"p0b_t{i}", [128, EC * 512], BF16) for i in range(2)]
        s_x = [sem("p0_xa"), sem("p0_xb")]
        s_dv = sem("p0_dv")
        s_ac = sem("p0_ac")
        s_sp = [sem("p0_spa"), sem("p0_spb")]
        s_ti = [sem("p0b_ia"), sem("p0b_ib")]
        s_to = [sem("p0b_oa"), sem("p0b_ob")]
        with nc.Block() as blk:
            @blk.sync
            def _(sy):
                def emit_tgroup(n):
                    sy.wait_ge(s_sp[0], 16 * (2 * n + 2))
                    sy.wait_ge(s_sp[1], 16 * (2 * n + 2))
                    if n >= 2:
                        sy.wait_ge(s_to[n % 2], 16 * (n // 2))
                    for e in range(EC):
                        sy.dma_start(
                            tb[n % 2][:, e * 512:(e + 1) * 512],
                            xn_tm[n * 512:(n + 1) * 512, e * 128:(e + 1) * 128],
                            transpose=True,
                        ).then_inc(s_ti[n % 2], 16)

                for i in range(NT0):
                    if i >= 2:
                        sy.wait_ge(s_ac, 3 * (i - 1))
                    sy.dma_start(xb[i % 2][:], x_full[i * 128:(i + 1) * 128, :]).then_inc(s_x[i % 2], 16)
                    if i >= 6 and (i - 6) % 4 == 0 and (i - 6) // 4 < NGT:
                        emit_tgroup((i - 6) // 4)
                for n in range((NT0 - 6 + 3) // 4, NGT):
                    emit_tgroup(n)

            @blk.vector
            def _(v):
                for i in range(NT0):
                    o = 8 * (i % 2)
                    stt = stt_t[:, o:o + 8]
                    v.wait_ge(s_x[i % 2], 16 * (i // 2 + 1))
                    if i >= 2:
                        v.wait_ge(s_dv, 6 * (i - 1))
                        v.wait_ge(s_ac, 3 * (i - 1))
                    v.scalar_tensor_tensor(sq[i % 2][:], xb[i % 2][:], c_one[:, 0:1], xb[i % 2][:],
                                           OP.mult, OP.mult, accum_out=stt[:, 0:1]).then_inc(s_dv, 1)
                    v.reduce_sum(stt[:, 1:2], xb[i % 2][:], axis=AX.X).then_inc(s_dv, 1)
                    v.wait_ge(s_dv, 6 * i + 2)
                    v.tensor_scalar_mul(stt[:, 2:3], stt[:, 1:2], c_invE[:, 0:1]).then_inc(s_dv, 1)
                    v.wait_ge(s_dv, 6 * i + 3)
                    v.tensor_mul(stt[:, 3:4], stt[:, 2:3], stt[:, 2:3]).then_inc(s_dv, 1)
                    v.wait_ge(s_dv, 6 * i + 4)
                    v.scalar_tensor_tensor(stt[:, 4:5], stt[:, 0:1], c_invE[:, 0:1], stt[:, 3:4],
                                           OP.mult, OP.subtract).then_inc(s_dv, 1)
                    v.wait_ge(s_ac, 3 * i + 2)
                    v.scalar_tensor_tensor(stt[:, 7:8], stt[:, 2:3], c_negone[:, 0:1], stt[:, 6:7],
                                           OP.mult, OP.mult).then_inc(s_dv, 1)

            @blk.scalar
            def _(s):
                for i in range(NT0):
                    o = 8 * (i % 2)
                    stt = stt_t[:, o:o + 8]
                    s.wait_ge(s_dv, 6 * i + 5)
                    s.activation(stt[:, 5:6], stt[:, 4:5], AF.Ln, bias=c_eps[:, 0:1]).then_inc(s_ac, 1)
                    s.wait_ge(s_ac, 3 * i + 1)
                    s.activation(stt[:, 6:7], stt[:, 5:6], AF.Exp, scale=c_nh[:, 0:1]).then_inc(s_ac, 1)
                    s.wait_ge(s_dv, 6 * i + 6)
                    s.wait_ge(s_x[i % 2], 16 * (i // 2 + 1))
                    if i >= 2:
                        s.wait_ge(s_sp[i % 2], 16 * (i // 2))
                    s.activation(xnb[i % 2][:], xb[i % 2][:], AF.Identity,
                                 bias=stt[:, 7:8], scale=stt[:, 6:7]).then_inc(s_ac, 1)
                    s.wait_ge(s_ac, 3 * i + 3)
                    s.dma_start(xn_tm[i * 128:(i + 1) * 128, :], xnb[i % 2][:]).then_inc(s_sp[i % 2], 16)

            @blk.gpsimd
            def _(g):
                g.dma_start(cs_sb[:], cs2[:, :]).then_inc(s_cs, 16)
                g.dma_start(ss_sb[:], ss2[:, :]).then_inc(s_cs, 16)
                for n in range(NGT):
                    g.wait_ge(s_ti[n % 2], 16 * EC * (n // 2 + 1))
                    g.dma_start(
                        xnT_view[:, :, n * 512:(n + 1) * 512],
                        tb[n % 2][:].rearrange("p (e t) -> p e t", t=512),
                    ).then_inc(s_to[n % 2], 16)
                g.wait_ge(s_cs, 32)
                g.wait_ge(s_to[0], 16 * ((NGT + 1) // 2))
                g.wait_ge(s_to[1], 16 * (NGT // 2))
                g.wait_ge(s_sp[0], 16 * ((NT0 + 1) // 2))
                g.wait_ge(s_sp[1], 16 * (NT0 // 2))

    def proj_rope_phase(tag, NB, wsrc, WBC, NTG, T, bias_sb, head_of, spill_fn):
        NPCL = WBC // 128
        NPIECE = NB * NTG * NPCL
        NWB = min(NB, 2)

        def pidx(b, tg, qc):
            return (b * NTG + tg) * NPCL + qc

        with ExitStack() as pc:
            sbl = lambda n, s, d: pc.enter_context(nc.sbuf_tensor(n, s, d))
            psl = lambda n, s: pc.enter_context(nc.psum_tensor(n, s, F32))
            wbuf = [sbl(f"{tag}_w{i}", [128, EC * WBC], BF16) for i in range(NWB)]
            xnT = [sbl(f"{tag}_xnT{i}", [128, EC * T], BF16) for i in range(2)]
            qraw = [sbl(f"{tag}_qraw{i}", [128, T], BF16) for i in range(2)]
            t1b = [sbl(f"{tag}_t1{i}", [128, T], F32) for i in range(2)]
            t2b = [sbl(f"{tag}_t2{i}", [128, T], F32) for i in range(2)]
            rp = [sbl(f"{tag}_rp{i}", [128, T], BF16) for i in range(2)]
            ps_q = [psl(f"{tag}_psq{i}", [128, T]) for i in range(2)]
            ps_w = [psl(f"{tag}_psw{i}", [128, T]) for i in range(2)]
            s_w = [sem(f"{tag}_wa"), sem(f"{tag}_wb")]
            s_xT = [sem(f"{tag}_xTa"), sem(f"{tag}_xTb")]
            s_mm = sem(f"{tag}_mm")
            s_ev = sem(f"{tag}_ev")
            s_pm = sem(f"{tag}_pm")
            s_rp = sem(f"{tag}_rp")
            s_qs = [sem(f"{tag}_qsa"), sem(f"{tag}_qsb")]

            with nc.Block() as blk:
                @blk.gpsimd
                def _(g):
                    for b in range(NB):
                        if b >= NWB:
                            g.wait_ge(s_mm, (b - NWB + 1) * NTG * NPCL)
                        g.dma_start(
                            wbuf[b % NWB][:].rearrange("p (c q) -> p c q", q=WBC),
                            wsrc[:, b * WBC:(b + 1) * WBC].rearrange("(c p) q -> p c q", p=128),
                        ).then_inc(s_w[b % 2], 16)

                @blk.sync
                def _(sy):
                    for b in range(NB):
                        for tg in range(NTG):
                            n = b * NTG + tg
                            if n >= 2:
                                sy.wait_ge(s_mm, (n - 1) * NPCL)
                            sy.dma_start(
                                xnT[n % 2][:].rearrange("p (e t) -> p e t", t=T),
                                xnT_view[:, :, tg * T:(tg + 1) * T],
                            ).then_inc(s_xT[n % 2], 16)

                @blk.tensor
                def _(t):
                    for b in range(NB):
                        t.wait_ge(s_w[b % 2], 16 * (b // 2 + 1))
                        for tg in range(NTG):
                            n = b * NTG + tg
                            t.wait_ge(s_xT[n % 2], 16 * (n // 2 + 1))
                            for qc in range(NPCL):
                                pi = pidx(b, tg, qc)
                                if pi >= 2:
                                    t.wait_ge(s_ev, pi - 1)
                                for e in range(EC):
                                    mm = t.matmul(
                                        ps_q[pi % 2][:],
                                        wbuf[b % NWB][:, e * WBC + qc * 128: e * WBC + (qc + 1) * 128],
                                        xnT[n % 2][:, e * T:(e + 1) * T],
                                        start=(e == 0), stop=(e == EC - 1))
                                mm.then_inc(s_mm, 1)
                                t.wait_ge(s_ev, pi + 1)
                                if pi >= 2:
                                    t.wait_ge(s_rp, 3 * (pi - 2) + 2)
                                t.matmul(ps_w[pi % 2][:], psw_sb[:], qraw[pi % 2][:],
                                         start=True, stop=True).then_inc(s_pm, 1)

                @blk.scalar
                def _(s):
                    for b in range(NB):
                        for tg in range(NTG):
                            for qc in range(NPCL):
                                pi = pidx(b, tg, qc)
                                hd_i = head_of(b, qc)
                                s.wait_ge(s_mm, pi + 1)
                                if pi >= 2:
                                    s.wait_ge(s_pm, pi - 1)
                                    s.wait_ge(s_rp, 3 * (pi - 2) + 1)
                                s.activation(qraw[pi % 2][:], ps_q[pi % 2][:], AF.Identity,
                                             bias=bias_sb[:, hd_i:hd_i + 1]).then_inc(s_ev, 1)
                                s.wait_ge(s_rp, 3 * pi + 3)
                                s.dma_start(spill_fn(b, tg, qc), rp[pi % 2][:]).then_inc(s_qs[pi % 2], 16)

                @blk.vector
                def _(v):
                    for b in range(NB):
                        for tg in range(NTG):
                            for qc in range(NPCL):
                                pi = pidx(b, tg, qc)
                                tok0 = tg * T
                                v.wait_ge(s_ev, pi + 1)
                                if pi >= 2:
                                    v.wait_ge(s_rp, 3 * (pi - 2) + 3)
                                v.tensor_mul(t1b[pi % 2][:], qraw[pi % 2][:],
                                             cs_sb[:, tok0:tok0 + T]).then_inc(s_rp, 1)
                                v.wait_ge(s_pm, pi + 1)
                                v.tensor_mul(t2b[pi % 2][:], ps_w[pi % 2][:],
                                             ss_sb[:, tok0:tok0 + T]).then_inc(s_rp, 1)
                                v.wait_ge(s_rp, 3 * pi + 2)
                                if pi >= 2:
                                    v.wait_ge(s_qs[pi % 2], 16 * (pi // 2))
                                v.tensor_add(rp[pi % 2][:], t1b[pi % 2][:], t2b[pi % 2][:]).then_inc(s_rp, 1)

                @blk.gpsimd
                def _(g):
                    g.wait_ge(s_qs[0], 16 * ((NPIECE + 1) // 2))
                    g.wait_ge(s_qs[1], 16 * (NPIECE // 2))

    if True:
        proj_rope_phase(
            "p1", NQB, wq, QB, P1G, P1T,
            bias_sb=bq_sb,
            head_of=lambda b, qc: (b * QB) // 128 + qc,
            spill_fn=lambda b, tg, qc: q_fm[(b * QB + qc * 128):(b * QB + (qc + 1) * 128),
                                            tg * P1T:(tg + 1) * P1T],
        )
        proj_rope_phase(
            "p2a", 1, wk, KVD, OWN // T2, T2,
            bias_sb=bk_sb,
            head_of=lambda b, qc: qc,
            spill_fn=lambda b, tg, qc: k_own[qc * 128:(qc + 1) * 128, tg * T2:(tg + 1) * T2],
        )
    cs_ctx.close()

    with ExitStack() as pc:
        sbl = lambda n, s, d: pc.enter_context(nc.sbuf_tensor(n, s, d))
        psl = lambda n, s: pc.enter_context(nc.psum_tensor(n, s, F32))
        wvb = sbl("p2b_wv", [128, EC * KVD], BF16)
        xnT = [sbl(f"p2b_xnT{i}", [128, EC * T2], BF16) for i in range(2)]
        vstg = [sbl(f"p2b_vs{i}", [128, KVD], BF16) for i in range(2)]
        ps_v = [psl(f"p2b_ps{i}", [128, KVD]) for i in range(2)]
        s_w = sem("p2b_w")
        s_xT = [sem("p2b_xTa"), sem("p2b_xTb")]
        s_mm = sem("p2b_mm")
        s_ev = sem("p2b_ev")
        s_vs = [sem("p2b_vsa"), sem("p2b_vsb")]
        s_cc = sem("p2b_cc")
        NTC = T2 // 128
        NVN = KVD // 512 if KVD >= 512 else 1
        VNW = KVD // NVN
        NG2B = OWN // T2
        NPV = NG2B * NTC

        with nc.Block() as blk:
            @blk.gpsimd
            def _(g):
                g.dma_start(
                    wvb[:].rearrange("p (c q) -> p c q", q=KVD),
                    wv[:, :].rearrange("(c p) q -> p c q", p=128),
                ).then_inc(s_w, 16)

            @blk.sync
            def _(sy):
                for n in range(NG2B):
                    if n >= 2:
                        sy.wait_ge(s_mm, (n - 1) * NTC)
                    sy.dma_start(
                        xnT[n % 2][:].rearrange("p (e t) -> p e t", t=T2),
                        xnT_view[:, :, n * T2:(n + 1) * T2],
                    ).then_inc(s_xT[n % 2], 16)

            @blk.tensor
            def _(t):
                t.wait_ge(s_w, 16)
                for n in range(NG2B):
                    t.wait_ge(s_xT[n % 2], 16 * (n // 2 + 1))
                    for tc in range(NTC):
                        pi = n * NTC + tc
                        if pi >= 2:
                            t.wait_ge(s_ev, pi - 1)
                        for vn in range(NVN):
                            for e in range(EC):
                                mm = t.matmul(
                                    ps_v[pi % 2][:, vn * VNW:(vn + 1) * VNW],
                                    xnT[n % 2][:, e * T2 + tc * 128: e * T2 + (tc + 1) * 128],
                                    wvb[:, e * KVD + vn * VNW: e * KVD + (vn + 1) * VNW],
                                    start=(e == 0), stop=(e == EC - 1))
                        mm.then_inc(s_mm, 1)

            @blk.vector
            def _(v):
                for n in range(NG2B):
                    for tc in range(NTC):
                        pi = n * NTC + tc
                        v.wait_ge(s_mm, pi + 1)
                        if pi >= 2:
                            v.wait_ge(s_vs[pi % 2], 16 * KV * (pi // 2))
                        v.tensor_add(vstg[pi % 2][:], ps_v[pi % 2][:], bv_b[:, :]).then_inc(s_ev, 1)

            @blk.scalar
            def _(s):
                for n in range(NG2B):
                    for tc in range(NTC):
                        pi = n * NTC + tc
                        row0 = n * T2 + tc * 128
                        s.wait_ge(s_ev, pi + 1)
                        for hh in range(KV):
                            s.dma_start(v_own[hh * OWN + row0: hh * OWN + row0 + 128, :],
                                        vstg[pi % 2][:, hh * 128:(hh + 1) * 128]
                                        ).then_inc(s_vs[pi % 2], 16)

            @blk.gpsimd
            def _(g):
                g.wait_ge(s_vs[0], 16 * KV * ((NPV + 1) // 2))
                g.wait_ge(s_vs[1], 16 * KV * (NPV // 2))


    AcT = sb("AcT", [128, EC * RR], BF16)
    BcT = sb("BcT", [RR, F], BF16)
    bias_b = sb("bias_b", [128, F], F32)
    NOC = F // 128
    with ExitStack() as pc:
        sbl = lambda n, s, d: pc.enter_context(nc.sbuf_tensor(n, s, d))
        psl = lambda n, s, d=F32: pc.enter_context(nc.psum_tensor(n, s, d))
        grow = sbl("f0_g", [1, NL], F32)
        brow = sbl("f0_br", [1, F], F32)
        lab = [sbl(f"f0_la{i}", [R, E], F32) for i in range(2)]
        lbb = [sbl(f"f0_lb{i}", [128, NOC * R], F32) for i in range(2)]
        Aacc = sbl("f0_Aacc", [R, E], F32)
        Bacc = sbl("f0_Bacc", [128, NOC * R], F32)
        Asel = sbl("f0_Asel", [R, E], BF16)
        Ash = sbl("f0_Ash", [R, E], BF16)
        Bsel = sbl("f0_Bsel", [128, NOC * R], BF16)
        Bsh = sbl("f0_Bsh", [128, NOC * R], BF16)
        gate_b = sbl("f0_gb", [128, NL], F32)
        ps_g = psl("f0_psg", [128, 512])
        ps_A = [psl(f"f0_psA{i}", [128, R], BF16) for i in range(2)]
        ps_B = [psl(f"f0_psB{i}", [R, 128], BF16) for i in range(2)]
        s_ld = sem("f0_ld")
        s_la = [sem("f0_laa"), sem("f0_lab")]
        s_lbs = [sem("f0_lba"), sem("f0_lbb")]
        s_dv = sem("f0_dv")
        s_mm = sem("f0_mm")
        s_ev = sem("f0_ev")
        base0 = 1 + F // 512
        baseB = base0 + NL + 2
        dvtot = baseB + NL + 2
        with nc.Block() as blk:
            @blk.sync
            def _(sy):
                sy.dma_start(grow[:], gate[0:1, :]).then_inc(s_ld, 16)
                sy.dma_start(brow[:], bff[0:1, :]).then_inc(s_ld, 16)
                for n in range(NL):
                    if n >= 2:
                        sy.wait_ge(s_dv, base0 + n - 1)
                    sy.dma_start(lab[n % 2][:], lA[n, :, :]).then_inc(s_la[n % 2], 16)
                sy.wait_ge(s_dv, base0 + NL - 1)
                sy.dma_start(lab[NL % 2][:], lAs[:, :]).then_inc(s_la[NL % 2], 16)
                for n in range(NL):
                    if n >= 2:
                        sy.wait_ge(s_dv, baseB + n - 1)
                    sy.dma_start(
                        lbb[n % 2][:].rearrange("p (c r) -> p c r", r=R),
                        lB[n:n + 1, :, :].rearrange("a (c p) r -> (a p) c r", p=128),
                    ).then_inc(s_lbs[n % 2], 16)
                sy.wait_ge(s_dv, baseB + NL - 1)
                sy.dma_start(
                    lbb[NL % 2][:].rearrange("p (c r) -> p c r", r=R),
                    lBs[:, :].rearrange("(c p) r -> p c r", p=128),
                ).then_inc(s_lbs[NL % 2], 16)

            @blk.tensor
            def _(t):
                t.wait_ge(s_ld, 32)
                t.matmul(ps_g[:, 0:NL], ones_row[:], grow[:, :], start=True, stop=True).then_inc(s_mm, 1)
                for j in range(F // 512):
                    t.wait_ge(s_dv, j + 1)
                    t.matmul(ps_g[:, 0:512], ones_row[:], brow[0:1, j * 512:(j + 1) * 512],
                             start=True, stop=True).then_inc(s_mm, 1)
                t.wait_ge(s_dv, base0 + NL + 2)
                for e in range(EC):
                    for k, srcb in ((0, Asel), (1, Ash)):
                        pi = 2 * e + k
                        if pi >= 2:
                            t.wait_ge(s_ev, pi - 1)
                        t.transpose(ps_A[pi % 2][:, 0:R], srcb[0:R, e * 128:(e + 1) * 128],
                                    idb_sb[0:R, 0:R]).then_inc(s_mm, 1)
                t.wait_ge(s_dv, dvtot)
                for oc in range(NOC):
                    for k, srcb in ((0, Bsel), (1, Bsh)):
                        pi = 2 * EC + 2 * oc + k
                        t.wait_ge(s_ev, pi - 1)
                        t.transpose(ps_B[pi % 2][:, :], srcb[:, oc * R:(oc + 1) * R],
                                    idb_sb[:]).then_inc(s_mm, 1)

            @blk.vector
            def _(v):
                v.wait_ge(s_mm, 1)
                v.tensor_copy(gate_b[:], ps_g[:, 0:NL]).then_inc(s_dv, 1)
                for j in range(F // 512):
                    v.wait_ge(s_mm, j + 2)
                    v.tensor_copy(bias_b[:, j * 512:(j + 1) * 512], ps_g[:, 0:512]).then_inc(s_dv, 1)
                for n in range(NL):
                    v.wait_ge(s_la[n % 2], 16 * (n // 2 + 1))
                    if n == 0:
                        v.wait_ge(s_dv, 1)
                        v.tensor_scalar_mul(Aacc[:], lab[0][:], gate_b[0:R, 0:1]).then_inc(s_dv, 1)
                    else:
                        v.wait_ge(s_dv, base0 + n)
                        v.scalar_tensor_tensor(Aacc[:], lab[n % 2][:], gate_b[0:R, n:n + 1],
                                               Aacc[:], OP.mult, OP.add).then_inc(s_dv, 1)
                v.wait_ge(s_dv, base0 + NL)
                v.tensor_copy(Asel[:], Aacc[:]).then_inc(s_dv, 1)
                v.wait_ge(s_la[NL % 2], 16 * (NL // 2 + 1))
                v.tensor_copy(Ash[:], lab[NL % 2][:]).then_inc(s_dv, 1)
                for n in range(NL):
                    v.wait_ge(s_lbs[n % 2], 16 * (n // 2 + 1))
                    if n == 0:
                        v.tensor_scalar_mul(Bacc[:], lbb[0][:], gate_b[:, 0:1]).then_inc(s_dv, 1)
                    else:
                        v.wait_ge(s_dv, baseB + n)
                        v.scalar_tensor_tensor(Bacc[:], lbb[n % 2][:], gate_b[:, n:n + 1],
                                               Bacc[:], OP.mult, OP.add).then_inc(s_dv, 1)
                v.wait_ge(s_dv, baseB + NL)
                v.tensor_copy(Bsel[:], Bacc[:]).then_inc(s_dv, 1)
                v.wait_ge(s_lbs[NL % 2], 16 * (NL // 2 + 1))
                v.tensor_copy(Bsh[:], lbb[NL % 2][:]).then_inc(s_dv, 1)
                for e in range(EC):
                    for k in (0, 1):
                        pi = 2 * e + k
                        v.wait_ge(s_mm, 1 + F // 512 + pi + 1)
                        v.tensor_copy(AcT[:, e * RR + k * R: e * RR + (k + 1) * R],
                                      ps_A[pi % 2][:, 0:R]).then_inc(s_ev, 1)
                for oc in range(NOC):
                    for k in (0, 1):
                        pi = 2 * EC + 2 * oc + k
                        v.wait_ge(s_mm, 1 + F // 512 + pi + 1)
                        v.tensor_copy(BcT[k * R:(k + 1) * R, oc * 128:(oc + 1) * 128],
                                      ps_B[pi % 2][:, :]).then_inc(s_ev, 1)

            @blk.gpsimd
            def _(g):
                rg = [[2 * b_, 2 * b_ + 1] for b_ in range(4)]
                g.collective_compute("AllGather", OP.bypass, replica_groups=rg,
                                     ins=[k_own.ap().opt()],
                                     outs=[k_ga.ap().opt()]).then_inc(s_cc, 1)
                g.collective_compute("AllGather", OP.bypass, replica_groups=rg,
                                     ins=[v_own.ap().opt()],
                                     outs=[v_ga.ap().opt()]).then_inc(s_cc, 1)
                g.wait_ge(s_ev, 2 * EC + 2 * NOC)
                g.wait_ge(s_cc, 2)

    NTT = OWN // 128
    TPG = NTR = QT // 128
    HN = H * NTR
    with ExitStack() as pc:
        sbl = lambda n, s, d: pc.enter_context(nc.sbuf_tensor(n, s, d))
        psl = lambda n, s: pc.enter_context(nc.psum_tensor(n, s, F32))
        qbuf = [sbl(f"p3_q{i}", [128, QT], BF16) for i in range(2)]
        kbuf = [sbl(f"p3_k{i}", [128, S], BF16) for i in range(2)]
        vbuf = [sbl(f"p3_v{i}", [128, KC * 128], BF16) for i in range(2)]
        PT = [sbl(f"p3_pt{i}", [128, 1024], BF16) for i in range(4)]
        s_lb = [sbl(f"p3_sl{i}", [1, QT], F32) for i in range(2)]
        ostg = [sbl(f"p3_o{i}", [128, QT], BF16) for i in range(2)]
        xb = [sbl(f"f1_x{i}", [128, E], F32) for i in range(2)]
        atb = [sbl(f"f1_at{i}", [128, E], BF16) for i in range(2)]
        anb = [sbl(f"f1_an{i}", [128, E], BF16) for i in range(2)]
        xnb = [sbl(f"f1_xn{i}", [128, E], BF16) for i in range(2)]
        fsq = [sbl(f"f1_sq{i}", [128, E], F32) for i in range(2)]
        fst_t = sbl("f1_st", [128, 16], F32)
        STP = [psl(f"p3_stp{i}", [128, 1024]) for i in range(2)]
        OL = [psl(f"p3_ol{i}", [128, 1024]) for i in range(2)]
        s_qld = [sem("p3_qlda"), sem("p3_qldb")]
        s_kvl = [sem("p3_kvla"), sem("p3_kvlb")]
        s_stp = sem("p3_stp")
        s_exp = sem("p3_exp")
        s_pv = sem("p3_pv")
        s_oev = sem("p3_oev")
        s_lcp = sem("p3_lcp")
        s_ltd = [sem("p3_ltda"), sem("p3_ltdb")]
        s_asp = [sem("p3_aspa"), sem("p3_aspb")]
        s_x = [sem("f1_xa"), sem("f1_xb")]
        s_at = [sem("f1_ata"), sem("f1_atb")]
        s_dv = sem("f1_dv")
        s_ac = sem("f1_ac")
        s_rc = sem("f1_rc")
        s_ow = [sem("f1_owa"), sem("f1_owb")]
        s_sp = [sem("f1_spa"), sem("f1_spb")]
        KCH = KC // 2
        v_view = v_ga.rearrange("(x h c p) d -> x h p c d", x=2, p=128, c=KCH)

        def Rof(k):
            return (k // TPG + 1) * H + (k % TPG) * 6

        f1q = {"sync": [], "scalar": [], "vector": []}

        def sched(eng, r, fn):
            f1q[eng].append((r, fn))

        for k in range(NTT):
            grp_k = k // TPG
            sub_k = k % TPG
            Rk = Rof(k)
            row0 = k * 128

            def it_xld(sy, k=k, row0=row0):
                if k >= 2:
                    sy.wait_ge(s_ac, 3 * (k - 1))
                    sy.wait_ge(s_ow[k % 2], 16 * (k // 2))
                sy.dma_start(xb[k % 2][:], x_full[row0:row0 + 128, :]).then_inc(s_x[k % 2], 16)

            def it_atld(sy, k=k, row0=row0, grp_k=grp_k):
                if k >= 2:
                    sy.wait_ge(s_dv, 8 * (k - 1) + 1)
                sy.wait_ge(s_asp[0], 16 * (H // 2) * (grp_k + 1))
                sy.wait_ge(s_asp[1], 16 * (H // 2) * (grp_k + 1))
                sy.dma_start(atb[k % 2][:], attn_fm[:, row0:row0 + 128],
                             transpose=True).then_inc(s_at[k % 2], 16)

            def it_xout(sy, k=k, row0=row0):
                sy.wait_ge(s_dv, 8 * k + 2)
                sy.dma_start(out[row0:row0 + 128, :], xb[k % 2][:]).then_inc(s_ow[k % 2], 16)

            def it_spl(sy, k=k, row0=row0):
                sy.wait_ge(s_ac, 3 * k + 3)
                sy.dma_start(xn2_tm[row0:row0 + 128, :], xnb[k % 2][:]).then_inc(s_sp[k % 2], 16)

            sched("sync", Rk, it_atld)
            sched("sync", Rk, it_xld)
            sched("sync", Rk + 4, it_xout)
            sched("sync", Rk + 6, it_spl)

            def it_ln(s, k=k):
                o = 8 * (k % 2)
                fst = fst_t[:, o:o + 8]
                s.wait_ge(s_dv, 8 * k + 7)
                s.activation(fst[:, 5:6], fst[:, 4:5], AF.Ln, bias=c_eps[:, 0:1]).then_inc(s_ac, 1)
                s.wait_ge(s_ac, 3 * k + 1)
                s.activation(fst[:, 6:7], fst[:, 5:6], AF.Exp, scale=c_nh[:, 0:1]).then_inc(s_ac, 1)

            def it_xn(s, k=k):
                o = 8 * (k % 2)
                fst = fst_t[:, o:o + 8]
                s.wait_ge(s_dv, 8 * k + 8)
                if k >= 2:
                    s.wait_ge(s_sp[k % 2], 16 * (k // 2))
                s.activation(xnb[k % 2][:], xb[k % 2][:], AF.Identity,
                             bias=fst[:, 7:8], scale=fst[:, 6:7]).then_inc(s_ac, 1)

            sched("scalar", Rk + 4, it_ln)
            sched("scalar", Rk + 5, it_xn)

            def it_muls(v, k=k, grp_k=grp_k, sub_k=sub_k):
                if k % TPG == 0:
                    v.wait_ge(s_ltd[0], 16 * NTR * (H // 2) * (grp_k + 1))
                    v.wait_ge(s_ltd[1], 16 * NTR * (H // 2) * (grp_k + 1))
                    v.reciprocal(rl_tok[:, grp_k * HN:(grp_k + 1) * HN],
                                 l_tok[:, grp_k * HN:(grp_k + 1) * HN]).then_inc(s_rc, 1)
                v.wait_ge(s_rc, grp_k + 1)
                v.wait_ge(s_at[k % 2], 16 * (k // 2 + 1))
                if k >= 2:
                    v.wait_ge(s_dv, 8 * (k - 1))
                    v.wait_ge(s_ac, 3 * (k - 1))
                for g in range(H):
                    col = grp_k * HN + g * NTR + sub_k
                    mu = v.tensor_scalar_mul(anb[k % 2][:, g * 128:(g + 1) * 128],
                                             atb[k % 2][:, g * 128:(g + 1) * 128],
                                             rl_tok[:, col:col + 1])
                mu.then_inc(s_dv, 1)

            def it_add(v, k=k):
                v.wait_ge(s_x[k % 2], 16 * (k // 2 + 1))
                if k >= 2:
                    v.wait_ge(s_ow[k % 2], 16 * (k // 2))
                v.wait_ge(s_dv, 8 * k + 1)
                v.tensor_add(xb[k % 2][:], xb[k % 2][:], anb[k % 2][:]).then_inc(s_dv, 1)

            def it_stt(v, k=k):
                o = 8 * (k % 2)
                fst = fst_t[:, o:o + 8]
                v.wait_ge(s_dv, 8 * k + 2)
                v.scalar_tensor_tensor(fsq[k % 2][:], xb[k % 2][:], c_one[:, 0:1], xb[k % 2][:],
                                       OP.mult, OP.mult, accum_out=fst[:, 0:1]).then_inc(s_dv, 1)
                v.reduce_sum(fst[:, 1:2], xb[k % 2][:], axis=AX.X).then_inc(s_dv, 1)

            def it_stats(v, k=k):
                o = 8 * (k % 2)
                fst = fst_t[:, o:o + 8]
                v.wait_ge(s_dv, 8 * k + 4)
                v.tensor_scalar_mul(fst[:, 2:3], fst[:, 1:2], c_invE[:, 0:1]).then_inc(s_dv, 1)
                v.wait_ge(s_dv, 8 * k + 5)
                v.tensor_mul(fst[:, 3:4], fst[:, 2:3], fst[:, 2:3]).then_inc(s_dv, 1)
                v.wait_ge(s_dv, 8 * k + 6)
                v.scalar_tensor_tensor(fst[:, 4:5], fst[:, 0:1], c_invE[:, 0:1], fst[:, 3:4],
                                       OP.mult, OP.subtract).then_inc(s_dv, 1)

            def it_nmr(v, k=k):
                o = 8 * (k % 2)
                fst = fst_t[:, o:o + 8]
                v.wait_ge(s_ac, 3 * k + 2)
                v.scalar_tensor_tensor(fst[:, 7:8], fst[:, 2:3], c_negone[:, 0:1], fst[:, 6:7],
                                       OP.mult, OP.mult).then_inc(s_dv, 1)

            sched("vector", Rk, it_muls)
            sched("vector", Rk + 1, it_add)
            sched("vector", Rk + 2, it_stt)
            sched("vector", Rk + 3, it_stats)
            sched("vector", Rk + 3, it_nmr)

        f1cur = {"sync": 0, "scalar": 0, "vector": 0}

        def pump(eng, handle, i):
            lst = f1q[eng]
            c = f1cur[eng]
            while c < len(lst) and lst[c][0] <= i:
                lst[c][1](handle)
                c += 1
            f1cur[eng] = c

        def flush(eng, handle):
            for r, fn in f1q[eng][f1cur[eng]:]:
                fn(handle)
            f1cur[eng] = len(f1q[eng])

        def spill_attn(sy, j):
            grpj, gj = divmod(j, H)
            sy.wait_ge(s_oev, j + 1)
            sy.dma_start(attn_fm[gj * 128:(gj + 1) * 128, grpj * QT:(grpj + 1) * QT],
                         ostg[j % 2][:]).then_inc(s_asp[j % 2], 16)

        with nc.Block() as blk:
            @blk.sync
            def _(sy):
                for i in range(NIT):
                    grp, g = divmod(i, H)
                    if i >= 2:
                        sy.wait_ge(s_stp, (i - 1) * PAIRS)
                    sy.dma_start(qbuf[i % 2][:],
                                 q_fm[g * 128:(g + 1) * 128,
                                      grp * QT:(grp + 1) * QT]).then_inc(s_qld[i % 2], 16)
                    if g % REP == 0:
                        hkv = g // REP
                        L = grp * KV + hkv
                        if L >= 2:
                            sy.wait_ge(s_pv, (L - 1) * REP * PAIRS)
                        sy.dma_start(kbuf[L % 2][:, 0:OWN],
                                     k_ga[hkv * 128:(hkv + 1) * 128, :]).then_inc(s_kvl[L % 2], 16)
                        sy.dma_start(kbuf[L % 2][:, OWN:S],
                                     k_ga[KVD + hkv * 128: KVD + (hkv + 1) * 128, :]).then_inc(s_kvl[L % 2], 16)
                        sy.dma_start(vbuf[L % 2][:, 0:KCH * 128].rearrange("p (c d) -> p c d", d=128),
                                     v_view[0, hkv]).then_inc(s_kvl[L % 2], 16)
                        sy.dma_start(vbuf[L % 2][:, KCH * 128:].rearrange("p (c d) -> p c d", d=128),
                                     v_view[1, hkv]).then_inc(s_kvl[L % 2], 16)
                    if i >= 1:
                        spill_attn(sy, i - 1)
                    pump("sync", sy, i)
                spill_attn(sy, NIT - 1)
                flush("sync", sy)

            @blk.tensor
            def _(t):
                def emit_pv(i, p, L):
                    G = i * PAIRS + p
                    t.wait_ge(s_exp, G + 1)
                    if p == 0 and i >= 2:
                        t.wait_ge(s_oev, i - 1)
                        t.wait_ge(s_lcp, i - 1)
                    for jj in (2 * p, 2 * p + 1):
                        pts = PT[G % 4][:, (jj % 2) * 512:(jj % 2) * 512 + QT]
                        t.matmul(OL[i % 2][:, 0:QT],
                                 vbuf[L % 2][:, jj * 128:(jj + 1) * 128],
                                 pts, start=(jj == 0), stop=(jj == KC - 1))
                        last = t.matmul(OL[i % 2][0:1, 512:512 + QT],
                                        ones_col[:], pts,
                                        start=(jj == 0), stop=(jj == KC - 1))
                    last.then_inc(s_pv, 1)

                for i in range(NIT):
                    grp, g = divmod(i, H)
                    hkv = g // REP
                    L = grp * KV + hkv
                    t.wait_ge(s_qld[i % 2], 16 * (i // 2 + 1))
                    if g % REP == 0:
                        t.wait_ge(s_kvl[L % 2], 64 * (L // 2 + 1))
                    for p in range(PAIRS):
                        G = i * PAIRS + p
                        if G >= 2:
                            t.wait_ge(s_exp, G - 1)
                        for jj in (2 * p, 2 * p + 1):
                            mm = t.matmul(STP[p % 2][:, (jj % 2) * 512:(jj % 2) * 512 + QT],
                                          kbuf[L % 2][:, jj * 128:(jj + 1) * 128],
                                          qbuf[i % 2][:],
                                          start=True, stop=True)
                        mm.then_inc(s_stp, 1)
                        if p >= 1:
                            emit_pv(i, p - 1, L)
                    emit_pv(i, PAIRS - 1, L)

            @blk.scalar
            def _(s):
                for i in range(NIT):
                    grp, g = divmod(i, H)
                    for p in range(PAIRS):
                        G = i * PAIRS + p
                        s.wait_ge(s_stp, G + 1)
                        if G >= 4:
                            s.wait_ge(s_pv, G - 3)
                        s.activation(PT[G % 4][:, 0:1024], STP[p % 2][:, 0:1024], AF.Exp,
                                     scale=c_scale[:, 0:1], bias=c_bias[:, 0:1]).then_inc(s_exp, 1)
                    pump("scalar", s, i)
                flush("scalar", s)

            @blk.vector
            def _(v):
                for i in range(NIT):
                    grp, g = divmod(i, H)
                    v.wait_ge(s_pv, (i + 1) * PAIRS)
                    if i >= 2:
                        v.wait_ge(s_asp[i % 2], 16 * (i // 2))
                    v.tensor_copy(ostg[i % 2][:], OL[i % 2][:, 0:QT]).then_inc(s_oev, 1)
                    if i >= 2:
                        v.wait_ge(s_ltd[i % 2], 16 * NTR * (i // 2))
                    v.tensor_copy(s_lb[i % 2][:], OL[i % 2][0:1, 512:512 + QT]).then_inc(s_lcp, 1)
                    pump("vector", v, i)
                flush("vector", v)

            @blk.gpsimd
            def _(g):
                for i in range(NIT):
                    grpi, gi = divmod(i, H)
                    g.wait_ge(s_lcp, i + 1)
                    col = grpi * HN + gi * NTR
                    for c in range(NTR):
                        g.dma_start(l_tok[:, col + c:col + c + 1],
                                    s_lb[i % 2][0:1, c * 128:(c + 1) * 128]
                                    ).then_inc(s_ltd[i % 2], 16)
                g.wait_ge(s_ltd[0], 16 * NTR * ((NIT + 1) // 2))
                g.wait_ge(s_ltd[1], 16 * NTR * (NIT // 2))
                g.wait_ge(s_asp[0], 16 * ((NIT + 1) // 2))
                g.wait_ge(s_asp[1], 16 * (NIT // 2))
                g.wait_ge(s_ow[0], 16 * ((NTT + 1) // 2))
                g.wait_ge(s_ow[1], 16 * (NTT // 2))
                g.wait_ge(s_sp[0], 16 * ((NTT + 1) // 2))
                g.wait_ge(s_sp[1], 16 * (NTT // 2))

    with ExitStack() as pc:
        sbl = lambda n, s, d: pc.enter_context(nc.sbuf_tensor(n, s, d))
        psl = lambda n, s: pc.enter_context(nc.psum_tensor(n, s, F32))
        xn2T = sbl("f3_xn2T", [128, EC * HALF], BF16)
        kb = [sbl(f"f3_kb{i}", [128, EC * FBW], BF16) for i in range(2)]
        AxT = sbl("f3_AxT", [RR, HALF], BF16)
        osb = [sbl(f"f3_o{i}", [128, FBW], F32) for i in range(2)]
        ps_f = [psl(f"f3_ps{i}", [128, FBW]) for i in range(2)]
        SW = min(512, HALF)
        ps_ax = psl("f3_psax", [RR, SW])
        s_xT = [sem("f3_xTa"), sem("f3_xTb")]
        s_kb = [sem("f3_kba"), sem("f3_kbb")]
        s_ax = sem("f3_ax")
        s_mm = sem("f3_mm")
        s_ev = sem("f3_ev")
        s_ga = [sem("f3_gaa"), sem("f3_gab")]
        NSEG = HALF // SW
        NPH = NFB * NTH
        NFB2 = 2 * NFB

        def kbl(g_or_sy, fbg):
            hf2, fb2 = divmod(fbg, NFB)
            g_or_sy.dma_start(
                kb[fbg % 2][:].rearrange("p (c q) -> p c q", q=FBW),
                wff[:, fb2 * FBW:(fb2 + 1) * FBW].rearrange("(c p) q -> p c q", p=128),
            ).then_inc(s_kb[fbg % 2], 16)

        with nc.Block() as blk:
            @blk.sync
            def _(sy):
                for hf in range(2):
                    if hf == 1:
                        sy.wait_ge(s_mm, NPH)
                        sy.wait_ge(s_ax, 3 * NSEG)
                    for e in range(EC):
                        sy.dma_start(
                            xn2T[:, e * HALF:(e + 1) * HALF],
                            xn2_tm[hf * HALF:(hf + 1) * HALF, e * 128:(e + 1) * 128],
                            transpose=True,
                        ).then_inc(s_xT[hf], 16)

            @blk.gpsimd
            def _(g):
                for fbg in range(min(2, NFB2)):
                    kbl(g, fbg)
                for hf in range(2):
                    for fb in range(NFB):
                        for ti in range(NTH):
                            n3 = (hf * NFB + fb) * NTH + ti
                            g.wait_ge(s_ev, n3 + 1)
                            row0 = hf * HALF + ti * 128
                            g.dma_start(out[row0:row0 + 128, fb * FBW:(fb + 1) * FBW],
                                        osb[n3 % 2][:],
                                        accum_op=OP.add).then_inc(s_ga[n3 % 2], 16)
                        fbg = hf * NFB + fb
                        if fbg + 2 < NFB2:
                            g.wait_ge(s_mm, (fbg + 1) * NTH)
                            kbl(g, fbg + 2)

            @blk.tensor
            def _(t):
                for hf in range(2):
                    t.wait_ge(s_xT[hf], 16 * EC)
                    for sg in range(NSEG):
                        sgg = hf * NSEG + sg
                        if sgg >= 1:
                            t.wait_ge(s_ax, 3 * sgg)
                        for e in range(EC):
                            mm = t.matmul(ps_ax[:, :],
                                          AcT[:, e * RR:(e + 1) * RR],
                                          xn2T[:, e * HALF + sg * SW: e * HALF + (sg + 1) * SW],
                                          start=(e == 0), stop=(e == EC - 1))
                        mm.then_inc(s_ax, 2)
                    t.wait_ge(s_ax, 3 * NSEG * (hf + 1))
                    for fb in range(NFB):
                        fbg = hf * NFB + fb
                        t.wait_ge(s_kb[fbg % 2], 16 * (fbg // 2 + 1))
                        for ti in range(NTH):
                            n3 = fbg * NTH + ti
                            if n3 >= 2:
                                t.wait_ge(s_ev, n3 - 1)
                            for e in range(EC):
                                t.matmul(ps_f[n3 % 2][:],
                                         xn2T[:, e * HALF + ti * 128: e * HALF + (ti + 1) * 128],
                                         kb[fbg % 2][:, e * FBW:(e + 1) * FBW],
                                         start=(e == 0), stop=False)
                            t.matmul(ps_f[n3 % 2][:],
                                     AxT[:, ti * 128:(ti + 1) * 128],
                                     BcT[:, fb * FBW:(fb + 1) * FBW],
                                     start=False, stop=True).then_inc(s_mm, 1)

            @blk.vector
            def _(v):
                for hf in range(2):
                    for sg in range(NSEG):
                        sgg = hf * NSEG + sg
                        v.wait_ge(s_ax, 3 * sgg + 2)
                        if hf == 1 and sg == 0:
                            v.wait_ge(s_mm, NPH)
                        v.tensor_copy(AxT[:, sg * SW:(sg + 1) * SW], ps_ax[:, :]).then_inc(s_ax, 1)
                    for fb in range(NFB):
                        for ti in range(NTH):
                            n3 = (hf * NFB + fb) * NTH + ti
                            v.wait_ge(s_mm, n3 + 1)
                            if n3 >= 2:
                                v.wait_ge(s_ga[n3 % 2], 16 * (n3 // 2))
                            v.tensor_add(osb[n3 % 2][:], ps_f[n3 % 2][:],
                                         bias_b[:, fb * FBW:(fb + 1) * FBW]).then_inc(s_ev, 1)

            @blk.gpsimd
            def _(g):
                npc = 2 * NPH
                g.wait_ge(s_ga[0], 16 * ((npc + 1) // 2))
                g.wait_ge(s_ga[1], 16 * (npc // 2))

    ctx.close()
    return nc


_PROG_CACHE = {}


def _get_program(cfg_key, cfg):
    if cfg_key not in _PROG_CACHE:
        _PROG_CACHE[cfg_key] = build_program(cfg)
    return _PROG_CACHE[cfg_key]


def _rope_tables(S, D, shift):
    inv = 10000.0 ** (-np.arange(0, D, 2, dtype=np.float64) / D)
    pos = (np.arange(S, dtype=np.int64) + shift) % S
    ang = pos[None, :].astype(np.float64) * inv[:, None]
    cos = np.cos(ang).astype(np.float32)
    sin = np.sin(ang).astype(np.float32)
    cs2 = np.concatenate([cos, cos], axis=0)
    ss2 = np.concatenate([-sin, sin], axis=0)
    return np.ascontiguousarray(cs2), np.ascontiguousarray(ss2)


def kernel(L0_attn_x, L0_attn_wq, L0_attn_bq, L0_attn_wk, L0_attn_bk,
           L0_attn_wv, L0_attn_bv, L0_ffn_gate, L0_ffn_kernel, L0_ffn_bias,
           L0_ffn_lora_A, L0_ffn_lora_B, L0_ffn_lora_A_shared,
           L0_ffn_lora_B_shared):
    x = np.asarray(L0_attn_x)
    B, S, E = x.shape
    H = np.asarray(L0_attn_wq).shape[1] // 128
    KV = np.asarray(L0_attn_wk).shape[1] // 128
    F = np.asarray(L0_ffn_kernel).shape[1]
    NL, R, _ = np.asarray(L0_ffn_lora_A).shape
    cfg = dict(S=S, E=E, H=H, KV=KV, D=128, F=F, NL=NL, R=R)
    OWN = S // 2
    nc = _get_program((S, E, H, KV, F, NL, R), cfg)

    f32c = lambda a: np.ascontiguousarray(np.asarray(a, dtype=np.float32))
    wq_ = f32c(L0_attn_wq); bq_ = f32c(L0_attn_bq).reshape(1, -1)
    wk_ = f32c(L0_attn_wk); bk_ = f32c(L0_attn_bk).reshape(1, -1)
    wv_ = f32c(L0_attn_wv); bv_ = f32c(L0_attn_bv).reshape(1, -1)
    wff_ = f32c(L0_ffn_kernel); bff_ = f32c(L0_ffn_bias).reshape(1, -1)
    gate_ = f32c(L0_ffn_gate)
    lA_ = f32c(L0_ffn_lora_A); lB_ = f32c(L0_ffn_lora_B)
    lAs_ = f32c(L0_ffn_lora_A_shared); lBs_ = f32c(L0_ffn_lora_B_shared)
    id_f = np.eye(128, dtype=np.float32)
    id_b = id_f.astype(ml_dtypes.bfloat16)
    psw = np.zeros((128, 128), dtype=np.float32)
    for m in range(128):
        psw[(m + 64) % 128, m] = 1.0
    psw = psw.astype(ml_dtypes.bfloat16)

    in_maps = []
    for c in range(8):
        b, h = divmod(c, 2)
        xc = np.ascontiguousarray(np.roll(f32c(x[b]), -h * OWN, axis=0))
        cs2_, ss2_ = _rope_tables(S, 128, h * OWN)
        in_maps.append({
            "x_full": xc, "wq": wq_, "bq": bq_, "wk": wk_, "bk": bk_,
            "wv": wv_, "bv": bv_, "gate": np.ascontiguousarray(gate_[c:c + 1, :]),
            "wff": wff_, "bff": bff_, "lA": lA_, "lB": lB_, "lAs": lAs_, "lBs": lBs_,
            "id_bf": id_b, "id_f32": id_f, "pswap": psw, "cs2": cs2_, "ss2": ss2_,
        })
    res = run_bass_kernel_spmd(nc, in_maps, core_ids=list(range(8)))
    y = np.empty((B, S, E), dtype=np.float32)
    for c in range(8):
        b, h = divmod(c, 2)
        y[b, h * OWN:(h + 1) * OWN, :] = res.results[c]["out"]
    return y
